# version 18
# baseline (speedup 1.0000x reference)
import math
import numpy as np
import ml_dtypes
from contextlib import ExitStack

import concourse.bass as bass
import concourse.mybir as mybir
from concourse.bass_utils import run_bass_kernel_spmd

F32 = mybir.dt.float32
BF16 = mybir.dt.bfloat16
I8 = mybir.dt.int8
AF = mybir.ActivationFunctionType
ALU = mybir.AluOpType
AX = mybir.AxisListType.X

SEQ = 8192
D = 2048
NH = 16
DH = 128
INTER = 8192
NCORE = 8
TOK = SEQ // NCORE
HPC = NH // NCORE
KC = D // 128
NQC = SEQ // 512
ISCALE = 1.0 / math.sqrt(DH)
TC = 512
NTT = TOK // 128


class Plan:

    ENGS = ("sync", "tensor", "vector", "scalar", "gpsimd")

    def __init__(self, nc, ctx):
        self.q = {e: [] for e in self.ENGS}
        self.cnt = {}
        self.handles = {}
        self._nc = nc
        self._ctx = ctx

    def sem(self, name):
        if name not in self.handles:
            self.handles[name] = self._ctx.enter_context(
                self._nc.semaphore("sem" + name))
            self.cnt[name] = 0
        return name

    def inc(self, sem, n=1):
        self.sem(sem)
        self.cnt[sem] += n
        return self.cnt[sem]

    def val(self, sem):
        self.sem(sem)
        return self.cnt[sem]

    def emit(self, eng, fn):
        self.q[eng].append(fn)

    def op(self, eng, make, sem=None, n=1, waits=()):
        target = self.inc(sem, n) if sem else None
        waits = tuple(waits)
        for ws, wv in waits:
            self.sem(ws)

        def fn(e, make=make, sem=sem, n=n, waits=waits):
            for ws, wv in waits:
                if wv > 0:
                    e.wait_ge(self.handles[ws], wv)
            ins = make(e)
            if sem is not None:
                ins.then_inc(self.handles[sem], n)

        self.emit(eng, fn)
        return target

    def wait(self, eng, sem, value):
        self.sem(sem)
        if value <= 0:
            return

        def fn(e, sem=sem, value=value):
            e.wait_ge(self.handles[sem], value)

        self.emit(eng, fn)

    def barrier(self, engs):
        vals = [(s, self.val(s)) for s in ("PE", "SE", "VE")]
        for eng in engs:
            for s, v in vals:
                self.wait(eng, s, v)

    def flush(self, block):
        for eng_name in self.ENGS:
            q = self.q[eng_name]
            if not q:
                continue
            getattr(block, eng_name)(lambda e, q=tuple(q): [f(e) for f in q])
            self.q[eng_name] = []


def build(taps=(), level=4, interleave=True):
    taps = set(taps)
    nc = bass.Bass("TRN2", target_bir_lowering=False, debug=False)

    x_sh = nc.dram_tensor("x_sh", [TOK, D], F32, kind="ExternalInput").ap()
    wq = nc.dram_tensor("wq", [128, HPC * KC * 128], F32, kind="ExternalInput").ap()
    wk = nc.dram_tensor("wk", [128, HPC * KC * 128], F32, kind="ExternalInput").ap()
    wv = nc.dram_tensor("wv", [128, HPC * KC * 128], F32, kind="ExternalInput").ap()
    wo = nc.dram_tensor("wo", [128, HPC * D], F32, kind="ExternalInput").ap()
    bq = nc.dram_tensor("bq", [128, HPC], F32, kind="ExternalInput").ap()
    bk = nc.dram_tensor("bk", [128, HPC], F32, kind="ExternalInput").ap()
    bv = nc.dram_tensor("bv", [128, HPC], F32, kind="ExternalInput").ap()
    bo_r = nc.dram_tensor("bo_r", [1, D], F32, kind="ExternalInput").ap()
    sg_i = nc.dram_tensor("sg_i", [128, 64], F32, kind="ExternalInput").ap()
    su_i = nc.dram_tensor("su_i", [128, 64], F32, kind="ExternalInput").ap()
    sd_r = nc.dram_tensor("sd_r", [1, D], F32, kind="ExternalInput").ap()
    wg_d = nc.dram_tensor("wg_d", [D, INTER], I8, kind="ExternalInput").ap()
    wu_d = nc.dram_tensor("wu_d", [D, INTER], I8, kind="ExternalInput").ap()
    wd_d = nc.dram_tensor("wd_d", [INTER, D], I8, kind="ExternalInput").ap()
    c_id_d = nc.dram_tensor("c_id", [128, 128], BF16, kind="ExternalInput").ap()
    c_idf_d = nc.dram_tensor("c_idf", [128, 128], F32, kind="ExternalInput").ap()
    c_on_c_d = nc.dram_tensor("c_on_c", [128, 1], BF16, kind="ExternalInput").ap()
    c_on_rb_d = nc.dram_tensor("c_on_rb", [1, 128], BF16, kind="ExternalInput").ap()
    c_on_rf_d = nc.dram_tensor("c_on_rf", [1, 128], F32, kind="ExternalInput").ap()
    c_mask_d = nc.dram_tensor("c_mask", [128, 4 * 512], BF16, kind="ExternalInput").ap()

    out_d = nc.dram_tensor("out", [TOK, D], F32, kind="ExternalOutput").ap()
    tap_d = {}
    for t in taps:
        shape = {"t_xn": [128, KC * TOK], "t_q": [128, HPC * SEQ],
                 "t_k": [128, HPC * SEQ], "t_vt": [128, HPC * SEQ],
                 "t_z": [128, HPC * SEQ], "t_rs": [TOK, D],
                 "t_sums": [1, 4 * 512], "t_dbg": [128, 2 * D],
                 "t_fuse": [128, (INTER // 128) * TC]}[t]
        dt = F32 if t == "t_sums" else BF16
        tap_d[t] = nc.dram_tensor(t, shape, dt, kind="ExternalOutput").ap()

    xn_ctb = nc.dram_tensor("xn_ctb", [D, TOK], BF16).ap()
    xn_all = nc.dram_tensor("xn_all", [NCORE * D, TOK], BF16).ap()
    rs_ctb = nc.dram_tensor("rs_ctb", [SEQ, D], BF16).ap()
    rs_sc = nc.dram_tensor("rs_sc", [TOK, D], BF16).ap()
    part_d = nc.dram_tensor("part_d", [TOK, D], F32).ap()
    bc_d = nc.dram_tensor("bc_d", [128, 2 * D], F32).ap()

    with ExitStack() as top:
        E = top.enter_context
        p = Plan(nc, top)
        block = E(nc.Block())

        c_id = E(nc.sbuf_tensor("c_id_s", [128, 128], BF16))
        c_idf = E(nc.sbuf_tensor("c_idf_s", [128, 128], F32))
        c_on_c = E(nc.sbuf_tensor("c_on_c_s", [128, 1], BF16))
        c_on_rb = E(nc.sbuf_tensor("c_on_rb_s", [1, 128], BF16))
        c_on_rf = E(nc.sbuf_tensor("c_on_rf_s", [1, 128], F32))
        c_bq = E(nc.sbuf_tensor("c_bq", [128, HPC], F32))
        c_bk = E(nc.sbuf_tensor("c_bk", [128, HPC], F32))
        c_bv = E(nc.sbuf_tensor("c_bv", [128, HPC], F32))
        c_bqs = E(nc.sbuf_tensor("c_bqs", [128, HPC], F32))
        c_sg = E(nc.sbuf_tensor("c_sg", [128, 64], F32))
        c_su = E(nc.sbuf_tensor("c_su", [128, 64], F32))
        xn_mine = E(nc.sbuf_tensor("xn_mine", [128, KC * TOK], BF16))

        for dst, src in ((c_id, c_id_d), (c_idf, c_idf_d), (c_on_c, c_on_c_d), (c_on_rb, c_on_rb_d),
                         (c_on_rf, c_on_rf_d), (c_bq, bq), (c_bk, bk), (c_bv, bv),
                         (c_sg, sg_i), (c_su, su_i)):
            p.op("sync", lambda e, d=dst, s=src: e.dma_start(out=d[:, :], in_=s),
                 sem="C", n=16)
        LD_CONST = p.val("C")

        with ExitStack() as ph0_ctx:
            rows = ph0_ctx.enter_context(nc.sbuf_tensor("ph0_rows", [1, 2 * D], F32))
            bc_tmp = ph0_ctx.enter_context(nc.sbuf_tensor("bc_tmp", [128, 2 * D], F32))
            p_tmp = ph0_ctx.enter_context(nc.psum_tensor("p_tmp", [128, D], F32))
            c_bo_bc = bc_tmp[:, 0:D]
            c_sd_bc = bc_tmp[:, D:2 * D]
            p.op("sync", lambda e: e.dma_start(out=rows[:, 0:D], in_=bo_r), sem="C", n=16)
            p.op("sync", lambda e: e.dma_start(out=rows[:, D:2 * D], in_=sd_r), sem="C", n=16)
            C_ALL = p.val("C")
            p.op("vector", lambda e: e.tensor_scalar_mul(c_bqs[:, :], c_bq[:, :], ISCALE),
                 sem="VE", waits=[("C", C_ALL)])
            for j in range(4):
                p.op("tensor", lambda e, j=j: e.matmul(
                    p_tmp[:, j * 512:(j + 1) * 512], c_on_rf[:, :],
                    rows[:, j * 512:(j + 1) * 512], start=True, stop=True),
                    sem="PE", waits=[("C", C_ALL)] if j == 0 else ())
            PE_BO = p.val("PE")
            p.op("scalar", lambda e: e.activation(c_bo_bc, p_tmp[:, :], AF.Copy),
                 sem="SE", waits=[("PE", PE_BO)])
            SE_BO = p.val("SE")
            for j in range(4):
                p.op("tensor", lambda e, j=j: e.matmul(
                    p_tmp[:, j * 512:(j + 1) * 512], c_on_rf[:, :],
                    rows[:, D + j * 512:D + (j + 1) * 512], start=True, stop=True),
                    sem="PE", waits=[("SE", SE_BO)] if j == 0 else ())
            PE_SD = p.val("PE")
            p.op("scalar", lambda e: e.activation(c_sd_bc, p_tmp[:, :], AF.Copy),
                 sem="SE", waits=[("PE", PE_SD)])
            SE_SD = p.val("SE")
            st_bc = p.op("sync", lambda e: e.dma_start(out=bc_d, in_=bc_tmp[:, :]),
                         sem="BCD", n=16, waits=[("SE", SE_SD)])

            with ExitStack() as ln_ctx:
                LE = ln_ctx.enter_context
                xs = LE(nc.sbuf_tensor("xs", [128, 2 * D], F32))
                xn_tm = LE(nc.sbuf_tensor("xn_tm", [128, D], BF16))
                sqj = LE(nc.sbuf_tensor("sqj", [128, D], BF16))
                ln_sc = LE(nc.sbuf_tensor("ln_sc", [128, 8], F32))
                p_xt = LE(nc.psum_tensor("p_xt", [128, D], BF16))
                xs_rel = [0, 0]
                tm_rel = 0
                pxt_rel = 0
                for xt in range(NTT):
                    sl = xt % 2
                    ld = p.op("sync", lambda e, xt=xt, sl=sl: e.dma_start(
                        out=xs[:, sl * D:(sl + 1) * D],
                        in_=x_sh[xt * 128:(xt + 1) * 128, :]), sem=f"X{sl}", n=16,
                        waits=[("SE", xs_rel[sl])])
                    xv = lambda: xs[:, sl * D:(sl + 1) * D]
                    v1 = p.op("vector", lambda e, sl=sl: e.tensor_reduce(
                        ln_sc[:, 0:1], xs[:, sl * D:(sl + 1) * D], AX, ALU.add),
                        sem="VE", waits=[(f"X{sl}", ld)])
                    s1 = p.op("scalar", lambda e, sl=sl: e.activation(
                        sqj[:, :], xs[:, sl * D:(sl + 1) * D], AF.Square,
                        accum_out=ln_sc[:, 2:3]), sem="SE",
                        waits=[(f"X{sl}", ld), ("VE", pxt_rel)])
                    v2 = p.op("vector", lambda e: e.tensor_scalar_mul(
                        ln_sc[:, 1:2], ln_sc[:, 0:1], 1.0 / D), sem="VE",
                        waits=[("VE", v1)])
                    v3 = p.op("vector", lambda e: e.tensor_scalar_mul(
                        ln_sc[:, 3:4], ln_sc[:, 2:3], 1.0 / D), sem="VE",
                        waits=[("SE", s1), ("VE", v2)])
                    v4 = p.op("vector", lambda e: e.tensor_tensor(
                        ln_sc[:, 4:5], ln_sc[:, 1:2], ln_sc[:, 1:2], ALU.mult),
                        sem="VE", waits=[("VE", v3)])
                    v5 = p.op("vector", lambda e: e.tensor_tensor(
                        ln_sc[:, 3:4], ln_sc[:, 3:4], ln_sc[:, 4:5], ALU.subtract),
                        sem="VE", waits=[("VE", v4)])
                    s2 = p.op("scalar", lambda e: e.activation(
                        ln_sc[:, 4:5], ln_sc[:, 3:4], AF.Ln, bias=np.float32(1e-5)), sem="SE",
                        waits=[("VE", v5)])
                    s3 = p.op("scalar", lambda e: e.activation(
                        ln_sc[:, 5:6], ln_sc[:, 4:5], AF.Exp, scale=np.float32(-0.5)), sem="SE",
                        waits=[("SE", s2)])
                    v6 = p.op("vector", lambda e: e.tensor_tensor(
                        ln_sc[:, 6:7], ln_sc[:, 1:2], ln_sc[:, 5:6], ALU.mult),
                        sem="VE", waits=[("SE", s3), ("VE", v5)])
                    v7 = p.op("vector", lambda e: e.tensor_scalar_mul(
                        ln_sc[:, 7:8], ln_sc[:, 6:7], -1.0), sem="VE",
                        waits=[("VE", v6)])
                    s4 = p.op("scalar", lambda e, sl=sl: e.activation(
                        xn_tm[:, :], xs[:, sl * D:(sl + 1) * D], AF.Identity,
                        bias=ln_sc[:, 7:8], scale=ln_sc[:, 5:6]), sem="SE",
                        waits=[("VE", v7), ("PE", tm_rel)])
                    xs_rel[sl] = s4
                    for k in range(KC):
                        p.op("tensor", lambda e, k=k: e.transpose(
                            p_xt[:, k * 128:(k + 1) * 128],
                            xn_tm[:, k * 128:(k + 1) * 128], c_id[:, :]),
                            sem="PE",
                            waits=([("SE", s4), ("VE", pxt_rel)] if k == 0 else ()))
                    tm_rel = p.val("PE")
                    pxt_rel = p.op("vector", lambda e, xt=xt: e.tensor_copy(
                        xn_mine.ap().rearrange("p (k t) -> p k t", k=KC)
                        [:, :, xt * 128:(xt + 1) * 128],
                        p_xt.ap().rearrange("p (k t) -> p k t", k=KC)),
                        sem="VE", waits=[("PE", tm_rel)])
                VE_XN = p.val("VE")
                st1 = p.op("sync", lambda e: e.dma_start(
                    out=xn_ctb.rearrange("(k p) t -> p k t", p=128),
                    in_=xn_mine.ap().rearrange("p (k t) -> p k t", k=KC)),
                    sem="STA", n=16, waits=[("VE", VE_XN)])
                if "t_xn" in taps:
                    p.op("sync", lambda e: e.dma_start(out=tap_d["t_xn"], in_=xn_mine[:, :]),
                         sem="TAP", n=16, waits=[("VE", VE_XN)])
                p.flush(block)

        def plan_ag():
            p.op("gpsimd", lambda e: e.collective_compute(
                "AllGather", ALU.bypass, replica_groups=[list(range(NCORE))],
                ins=[xn_ctb.opt()], outs=[xn_all.opt()]), sem="CCA", n=1,
                waits=[("STA", st1)])

        if level >= 1 and level < 4:
            plan_ag()

        def emit_ffn(tcs, add_rs=False, bar=None, ag_hook=None):
            sfx = str(tcs[0])
            hookbox = [ag_hook]
            with ExitStack() as ffn_ctx:
              FE = ffn_ctx.enter_context
              wstream = FE(nc.sbuf_tensor("wstream" + sfx, [128, 3 * 8192], BF16))
              xld = FE(nc.sbuf_tensor("xld" + sfx, [128, 2 * D], F32))
              bcs = FE(nc.sbuf_tensor("bcs" + sfx, [128, 2 * D], F32))
              rs_t = FE(nc.sbuf_tensor("rs_t" + sfx, [128, D], BF16)) if add_rs else None
              fuse = FE(nc.sbuf_tensor("fuse" + sfx, [128, (INTER // 128) * TC], BF16))
              u_sb = FE(nc.sbuf_tensor("u_sb" + sfx, [128, 8 * TC], BF16))
              out_sb = FE(nc.sbuf_tensor("out_sb" + sfx, [128, 2 * D], F32))

              if bar is None:
                  p.barrier(["gpsimd", "sync", "tensor", "scalar", "vector"])
              else:
                  for eng in ("gpsimd", "sync"):
                      for s, v in bar:
                          p.wait(eng, s, v)
                  p.barrier(["tensor", "scalar", "vector"])
              for eng in ("scalar", "vector"):
                  for s in ("SP0", "SP1", "SR0", "SR1", "TAP"):
                      p.wait(eng, s, p.val(s))
              ldbc = p.op("sync", lambda e: e.dma_start(out=bcs[:, :], in_=bc_d),
                          sem=f"BCL{sfx}", n=16, waits=[("BCD", p.val("BCD"))])
              wst_rel = [0, 0, 0]
              pf_rel = [0, 0]
              gu_rel = 0
              fuse_rel = 0
              xld_rel = [0, 0]
              rst_rel = [0]
              out_rel = [0, 0]
              wslot = 0
              for tc in tcs:
                  with nc.psum_tensor(f"p_g{tc}", [128, 8 * 512], F32) as p_g:
                      ev_rel = [0] * 8
                      for sl in range(8):
                          for wi, (wdr, sc_ap, dst) in enumerate(
                                  ((wg_d, c_sg, None), (wu_d, c_su, u_sb))):
                              lds = []
                              for kh in range(2):
                                  ws = wslot
                                  wslot = (wslot + 1) % 3
                                  lds.append((ws, p.op("gpsimd",
                                      lambda e, wdr=wdr, kh=kh, sl=sl, ws=ws: e.dma_start(
                                          out=wstream.ap().rearrange(
                                              "p (b k f) -> p b k f", b=3, k=8)
                                          [:, ws, :, :],
                                          in_=wdr.rearrange("(kc p) f -> p kc f", p=128)
                                          [:, kh * 8:(kh + 1) * 8, sl * 1024:(sl + 1) * 1024]),
                                      sem=f"W{ws}", n=16,
                                      waits=([("PE", wst_rel[ws])] if wst_rel[ws]
                                             else [("BCD", p.val("BCD"))]))))
                              if hookbox[0] is not None:
                                  hookbox[0]()
                                  hookbox[0] = None
                              for kh in range(2):
                                  ws, ld = lds[kh]
                                  for m in range(8):
                                      for k in range(8):
                                          wts = []
                                          if m == 0 and k == 0:
                                              wts.append((f"W{ws}", ld))
                                          if k == 0 and m == 0 and kh == 0:
                                              if wi == 0:
                                                  wts.append(("VE", gu_rel))
                                                  if sl == 0:
                                                      wts.append(("VE", pf_rel[0]))
                                                      wts.append(("VE", pf_rel[1]))
                                          if k == 0 and kh == 0 and wi == 1:
                                              wts.append(("SE", ev_rel[m]))
                                          last_half = (m == 7 and k == 7)
                                          last_bank = (kh == 1 and k == 7)
                                          p.op("tensor",
                                              lambda e, ws=ws, k=k, m=m, kh=kh, tc=tc:
                                              e.matmul(
                                                  p_g[:, m * 512:(m + 1) * 512],
                                                  wstream.ap().rearrange(
                                                      "p (b k f) -> p b k f", b=3, k=8)
                                                  [:, ws, k, m * 128:(m + 1) * 128],
                                                  xn_mine[:, (kh * 8 + k) * 1024 + tc * TC:
                                                          (kh * 8 + k) * 1024 + tc * TC + TC],
                                                  start=(kh == 0 and k == 0),
                                                  stop=(kh == 1 and k == 7)),
                                              sem="PE" if (last_half or last_bank) else None,
                                              waits=wts)
                                      if kh == 1:
                                          pe_m = p.val("PE")
                                          if wi == 0:
                                              ev_rel[m] = p.op("scalar",
                                                  lambda e, m=m, sl=sl: e.activation(
                                                      fuse[:, (sl * 8 + m) * TC:
                                                           (sl * 8 + m + 1) * TC],
                                                      p_g[:, m * 512:(m + 1) * 512], AF.Silu,
                                                      scale=c_sg[:, sl * 8 + m:sl * 8 + m + 1]),
                                                  sem="SE",
                                                  waits=[("PE", pe_m)] + ([("VE", gu_rel)] if m == 0 else []))
                                          else:
                                              ev_rel[m] = p.op("scalar",
                                                  lambda e, m=m, sl=sl: e.activation(
                                                      u_sb[:, m * TC:(m + 1) * TC],
                                                      p_g[:, m * 512:(m + 1) * 512], AF.Identity,
                                                      scale=c_su[:, sl * 8 + m:sl * 8 + m + 1]),
                                                  sem="SE", waits=[("PE", pe_m)])
                                  wst_rel[ws] = p.val("PE")
                              pe_gu = p.val("PE")
                          gu_rel = p.op("vector", lambda e, sl=sl: e.tensor_tensor(
                              fuse[:, sl * 8 * TC:(sl + 1) * 8 * TC],
                              fuse[:, sl * 8 * TC:(sl + 1) * 8 * TC],
                              u_sb[:, :], ALU.mult), sem="VE",
                              waits=[("SE", ev_rel[7]), ("PE", fuse_rel)])
                  VE_FUSE = p.val("VE")
                  if tc == 0 and "t_fuse" in taps:
                      p.op("sync", lambda e: e.dma_start(out=tap_d["t_fuse"], in_=fuse[:, :]),
                           sem="ST", n=16, waits=[("VE", VE_FUSE)])

                  with nc.psum_tensor(f"p_f{tc}", [128, 2 * D], F32) as p_f:
                      for pr in range(2):
                          tiles = (2 * pr, 2 * pr + 1)
                          ldx = {}
                          for tt in tiles:
                              slx = tt % 2
                              ldx[tt] = p.op("sync", lambda e, tc=tc, tt=tt, slx=slx:
                                  e.dma_start(out=xld[:, slx * D:(slx + 1) * D],
                                              in_=x_sh[tc * TC + tt * 128:
                                                       tc * TC + tt * 128 + 128, :]),
                                  sem=f"XL{slx}", n=16, waits=[("VE", xld_rel[slx])])
                          for mg in range(16):
                              ws = wslot
                              wslot = (wslot + 1) % 3
                              ld = p.op("gpsimd", lambda e, mg=mg, ws=ws: e.dma_start(
                                  out=wstream.ap().rearrange("p (b m f) -> p b m f",
                                                             b=3, m=4)[:, ws, :, :],
                                  in_=wd_d.rearrange("(mc p) f -> p mc f", p=128)
                                  [:, mg * 4:(mg + 1) * 4, :]), sem=f"W{ws}", n=16,
                                  waits=[("PE", wst_rel[ws])])
                              for mi in range(4):
                                  m = mg * 4 + mi
                                  for ti, tt in enumerate(tiles):
                                      for nn in range(4):
                                          wts = []
                                          if mi == 0 and ti == 0 and nn == 0:
                                              wts.append((f"W{ws}", ld))
                                              if mg == 0:
                                                  wts.append(("VE", VE_FUSE))
                                          if mi == 0 and nn == 0 and mg == 0:
                                              wts.append(("VE", pf_rel[ti]))
                                          p.op("tensor", lambda e, ws=ws, mi=mi, m=m, ti=ti,
                                              tt=tt, nn=nn, tc=tc: e.matmul(
                                                  p_f[:, ti * D + nn * 512:
                                                      ti * D + nn * 512 + 512],
                                                  fuse[:, m * TC + tt * 128:
                                                       m * TC + tt * 128 + 128],
                                                  wstream.ap().rearrange(
                                                      "p (b m f) -> p b m f", b=3, m=4)
                                                  [:, ws, mi, nn * 512:(nn + 1) * 512],
                                                  start=(m == 0), stop=(m == 63)),
                                              sem="PE" if (mi == 3 and ti == 1 and nn == 3)
                                              else None,
                                              waits=wts)
                              wst_rel[ws] = p.val("PE")
                          fuse_rel = p.val("PE")
                          pe_pair = p.val("PE")
                          for ti, tt in enumerate(tiles):
                              slx = tt % 2
                              slo = tt % 2
                              osem = f"SO{slo}" if add_rs else f"SP{slo}"
                              v1 = p.op("vector", lambda e, ti=ti, slo=slo: e.tensor_tensor(
                                  out_sb[:, slo * D:(slo + 1) * D],
                                  p_f[:, ti * D:(ti + 1) * D], bcs[:, D:2 * D], ALU.mult),
                                  sem="VE", waits=[("PE", pe_pair), (osem, out_rel[slo]),
                                                   (f"BCL{sfx}", ldbc)])
                              v2 = p.op("vector", lambda e, slo=slo: e.tensor_tensor(
                                  out_sb[:, slo * D:(slo + 1) * D],
                                  out_sb[:, slo * D:(slo + 1) * D], bcs[:, 0:D], ALU.add),
                                  sem="VE", waits=[("VE", v1)])
                              v3 = p.op("vector", lambda e, slo=slo, slx=slx: e.tensor_tensor(
                                  out_sb[:, slo * D:(slo + 1) * D],
                                  out_sb[:, slo * D:(slo + 1) * D],
                                  xld[:, slx * D:(slx + 1) * D], ALU.add),
                                  sem="VE", waits=[("VE", v2), (f"XL{slx}", ldx[tt])])
                              xld_rel[slx] = v3
                              if add_rs:
                                  ldr = p.op("sync", lambda e, tc=tc, tt=tt: e.dma_start(
                                      out=rs_t[:, :],
                                      in_=rs_sc[tc * TC + tt * 128:
                                                tc * TC + tt * 128 + 128, :]),
                                      sem="RST", n=16,
                                      waits=[("CCR", p.val("CCR")), ("VE", rst_rel[0])])
                                  v3 = p.op("vector", lambda e, slo=slo: e.tensor_tensor(
                                      out_sb[:, slo * D:(slo + 1) * D],
                                      out_sb[:, slo * D:(slo + 1) * D],
                                      rs_t[:, :], ALU.add),
                                      sem="VE", waits=[("VE", v3), ("RST", ldr)])
                                  rst_rel[0] = v3
                              pf_rel[ti] = v3
                              dsttn = out_d if add_rs else part_d
                              dsem = f"SO{slo}" if add_rs else f"SP{slo}"
                              out_rel[slo] = p.op("sync", lambda e, tc=tc, tt=tt, slo=slo,
                                  dsttn=dsttn: e.dma_start(
                                      out=dsttn[tc * TC + tt * 128:
                                                tc * TC + tt * 128 + 128, :],
                                      in_=out_sb[:, slo * D:(slo + 1) * D]),
                                  sem=dsem, n=16, waits=[("VE", v3)])



        if level >= 4:
            emit_ffn([0], ag_hook=plan_ag if level >= 1 else None)
        CC_AG = p.val("CCA")

        with ExitStack() as big_ctx:
            BE = big_ctx.enter_context
            q_sb = BE(nc.sbuf_tensor("q_sb", [128, HPC * SEQ], BF16))
            k_sb = BE(nc.sbuf_tensor("k_sb", [128, HPC * SEQ], BF16))
            vt_sb = BE(nc.sbuf_tensor("vt_sb", [128, HPC * SEQ], BF16))


            if level >= 1:
              with ExitStack() as qkv_ctx:
                QE = qkv_ctx.enter_context
                wq_sb = QE(nc.sbuf_tensor("wq_sb", [128, HPC * KC * 128], BF16))
                wk_sb = QE(nc.sbuf_tensor("wk_sb", [128, HPC * KC * 128], BF16))
                wv_sb = QE(nc.sbuf_tensor("wv_sb", [128, HPC * KC * 128], BF16))
                xnb = QE(nc.sbuf_tensor("xnb", [128, 2 * KC * 512], BF16))
                vstage = QE(nc.sbuf_tensor("vstage", [128, HPC * 512], BF16))
                p_q = QE(nc.psum_tensor("p_q", [128, HPC * 512], F32))
                p_k = QE(nc.psum_tensor("p_k", [128, HPC * 512], F32))
                p_v = QE(nc.psum_tensor("p_v", [128, HPC * 512], F32))
                p_vt = QE(nc.psum_tensor("p_vt", [128, 512], BF16))
                p.barrier(["gpsimd", "sync"])
                for dst, src in ((wq_sb, wq), (wk_sb, wk), (wv_sb, wv)):
                    p.op("gpsimd", lambda e, d=dst, s=src: e.dma_start(
                        out=d[:, :], in_=s), sem="WQKV", n=16)
                GLD_W = p.val("WQKV")

                psum_of = {"q": p_q, "k": p_k, "v": p_v}
                w_of = {"q": wq_sb, "k": wk_sb, "v": wv_sb}
                proj_rel = {(pr, s): 0 for pr in "qkv" for s in range(HPC)}
                vst_rel = [0, 0]
                xnb_rel = [0, 0]
                for tb in range(NQC):
                    r, co = tb // 2, (tb % 2) * 512
                    xb = tb % 2
                    ld = p.op("sync", lambda e, r=r, co=co, xb=xb: e.dma_start(
                        out=xnb.ap().rearrange("p (b k t) -> p b k t", b=2, k=KC)
                        [:, xb, :, :],
                        in_=xn_all.rearrange("(r k p) t -> r p k t", k=KC, p=128)
                        [r, :, :, co:co + 512]), sem=f"XNB{xb}", n=16,
                        waits=[("CCA", CC_AG), ("PE", xnb_rel[xb])])
                    for pr in ("q", "k", "v"):
                        for s in range(HPC):
                            for k in range(KC):
                                wts = [(f"XNB{xb}", ld), ("WQKV", GLD_W),
                                       ("SE", proj_rel[(pr, s)])] if k == 0 else ()
                                p.op("tensor", lambda e, pr=pr, s=s, k=k, xb=xb: e.matmul(
                                    psum_of[pr][:, s * 512:(s + 1) * 512],
                                    w_of[pr][:, (s * KC + k) * 128:(s * KC + k) * 128 + 128],
                                    xnb[:, (xb * KC + k) * 512:(xb * KC + k) * 512 + 512],
                                    start=(k == 0), stop=(k == KC - 1)),
                                    sem="PE" if k == KC - 1 else None, waits=wts)
                            pe_done = p.val("PE")
                            if pr == "q":
                                proj_rel[(pr, s)] = p.op("scalar",
                                    lambda e, s=s, tb=tb: e.activation(
                                        q_sb[:, s * SEQ + tb * 512:s * SEQ + tb * 512 + 512],
                                        p_q[:, s * 512:(s + 1) * 512], AF.Identity,
                                        bias=c_bqs[:, s:s + 1], scale=np.float32(ISCALE)),
                                    sem="SE", waits=[("PE", pe_done), ("VE", p.val("VE"))])
                            elif pr == "k":
                                proj_rel[(pr, s)] = p.op("scalar",
                                    lambda e, s=s, tb=tb: e.activation(
                                        k_sb[:, s * SEQ + tb * 512:s * SEQ + tb * 512 + 512],
                                        p_k[:, s * 512:(s + 1) * 512], AF.Identity,
                                        bias=c_bk[:, s:s + 1]),
                                    sem="SE", waits=[("PE", pe_done)])
                            else:
                                vwts = [("PE", pe_done), ("VE", vst_rel[s])]
                                if tb == 0 and s == 0:
                                    vwts += [("SP0", p.val("SP0")),
                                             ("SP1", p.val("SP1"))]
                                proj_rel[(pr, s)] = p.op("scalar",
                                    lambda e, s=s: e.activation(
                                        vstage[:, s * 512:(s + 1) * 512],
                                        p_v[:, s * 512:(s + 1) * 512], AF.Identity,
                                        bias=c_bv[:, s:s + 1]),
                                    sem="SE", waits=vwts)
                    xnb_rel[xb] = p.val("PE")
                    for s in range(HPC):
                        se_v = proj_rel[("v", s)]
                        for i in range(4):
                            p.op("tensor", lambda e, s=s, i=i: e.transpose(
                                p_vt[:, i * 128:(i + 1) * 128],
                                vstage[:, s * 512 + i * 128:s * 512 + i * 128 + 128],
                                c_id[:, :]), sem="PE" if i == 3 else None,
                                waits=([("SE", se_v), ("VE", p.val("VE"))] if i == 0 else ()))
                        tpe = p.val("PE")
                        vst_rel[s] = p.op("vector", lambda e, s=s, tb=tb: e.tensor_copy(
                            vt_sb[:, (s * 64 + tb * 4) * 128:(s * 64 + tb * 4) * 128 + 512],
                            p_vt[:, :]), sem="VE", waits=[("PE", tpe)])
                if "t_q" in taps:
                    p.op("sync", lambda e: e.dma_start(out=tap_d["t_q"], in_=q_sb[:, :]),
                         sem="TAP", n=16, waits=[("SE", p.val("SE"))])
                if "t_k" in taps:
                    p.op("sync", lambda e: e.dma_start(out=tap_d["t_k"], in_=k_sb[:, :]),
                         sem="TAP", n=16, waits=[("SE", p.val("SE"))])
                if "t_vt" in taps:
                    p.op("sync", lambda e: e.dma_start(out=tap_d["t_vt"], in_=vt_sb[:, :]),
                         sem="TAP", n=16, waits=[("VE", p.val("VE"))])
                p.flush(block)

            z_stack = ExitStack()
            if level >= 2:
                z_sb = z_stack.enter_context(
                    nc.sbuf_tensor("z_sb", [128, HPC * SEQ], BF16))

            if level >= 2:
              with ExitStack() as attn_ctx:
                AE = attn_ctx.enter_context
                pt_sb = AE(nc.sbuf_tensor("pt_sb", [128, 4 * 512], BF16))
                c_mask_s = AE(nc.sbuf_tensor("c_mask_s", [128, 4 * 512], BF16))
                mstats = AE(nc.sbuf_tensor("mstats", [128, 64], BF16))
                mcol = AE(nc.sbuf_tensor("mcol", [128, 1], F32))
                negm = AE(nc.sbuf_tensor("negm", [1, 2], F32))
                negmb = AE(nc.sbuf_tensor("negmb", [1, 2], BF16))
                biasM = AE(nc.sbuf_tensor("biasM", [128, 2], F32))
                sumr = AE(nc.sbuf_tensor("sumr", [1, 512], F32))
                rsum = AE(nc.sbuf_tensor("rsum", [1, 512], F32))
                bcsum = AE(nc.sbuf_tensor("bcsum", [128, 512], F32))
                sums_hist = AE(nc.sbuf_tensor("sums_hist", [1, 4 * 512], F32)) \
                    if "t_sums" in taps else None
                sums_slots = {14: 0, 15: 1, 30: 2, 31: 3}
                p_sa = AE(nc.psum_tensor("p_sa", [128, 1024], F32))
                p_st = AE(nc.psum_tensor("p_st", [128, 3 * 512], F32))
                p_z = AE(nc.psum_tensor("p_z", [128, 2 * 512], F32))
                p_row = AE(nc.psum_tensor("p_row", [33, 512], F32))

                p.barrier(["sync", "tensor", "scalar", "vector"])
                p.op("sync", lambda e: e.dma_start(out=c_mask_s[:, :], in_=c_mask_d),
                     sem="MASK", n=16)
                LD_MASK = p.val("MASK")

                state = {
                    "sa_rel": [0, 0],
                    "st_rel": [0, 0, 0],
                    "pt_rel": [0, 0, 0, 0],
                    "pz_rel": [("VE", 0), ("VE", 0)],
                    "prow_rel": [0, 0],
                    "pbc_rel": 0,
                    "mstats_rel": 0,
                    "biasM_ready": [0, 0],
                    "mcol_rel": 0,
                    "negmb_rel": [0, 0],
                    "mask_ok": LD_MASK,
                }

                def passA_units(i):
                    s, qc = divmod(i, NQC)
                    units = []
                    nblk = 0
                    blocks = []
                    for qt in range(4):
                        for kb in range(qc + 1):
                            blocks.append((qt, kb))
                    bi = 0
                    while bi < len(blocks):
                        pair = blocks[bi:bi + 2]
                        bi += len(pair)

                        def unit(pair=pair, i=i, s=s, qc=qc, base=nblk):
                            pe_vals = []
                            for j, (qt, kb) in enumerate(pair):
                                v = p.op("tensor", lambda e, s=s, qc=qc, qt=qt, kb=kb, j=j:
                                    e.matmul(p_sa[:, j * 512:(j + 1) * 512],
                                             q_sb[:, s * SEQ + qc * 512 + qt * 128:
                                                  s * SEQ + qc * 512 + qt * 128 + 128],
                                             k_sb[:, s * SEQ + kb * 512:
                                                  s * SEQ + kb * 512 + 512],
                                             start=True, stop=True),
                                    sem="PE",
                                    waits=[("VE", state["sa_rel"][j])])
                                pe_vals.append(v)
                            w = max(pe_vals)
                            width = 512 * len(pair)
                            r = p.op("vector", lambda e, base=base, width=width:
                                e.tensor_reduce(mstats[:, base:base + 1],
                                                p_sa[:, 0:width], AX, ALU.max),
                                sem="VE", waits=[("PE", w), ("VE", state["mstats_rel"])])
                            state["sa_rel"][0] = r
                            if len(pair) > 1:
                                state["sa_rel"][1] = r

                        units.append(unit)
                        nblk += 1

                    def finalize(i=i, nblk=nblk):
                        sl = i % 2
                        v1 = p.op("vector", lambda e, nblk=nblk: e.tensor_reduce(
                            mcol[:, :], mstats[:, 0:nblk], AX, ALU.max), sem="VE",
                            waits=[("VE", p.val("VE")), ("PE", state["mcol_rel"])])
                        state["mstats_rel"] = v1
                        t1 = p.op("tensor", lambda e: e.transpose(
                            p_sa[0:1, 0:128], mcol[:, :], c_idf[:, :]), sem="PE",
                            waits=([("C", C_ALL)] if i == 0 else []) +
                            [("VE", v1), ("VE", state["sa_rel"][0])])
                        state["mcol_rel"] = t1
                        v2 = p.op("vector", lambda e, sl=sl: e.tensor_reduce(
                            negm[:, sl:sl + 1], p_sa[0:1, 0:128], AX, ALU.max,
                            negate=True),
                            sem="VE", waits=[("PE", t1)])
                        state["sa_rel"][0] = v2
                        v3 = p.op("vector", lambda e, sl=sl: e.tensor_copy(
                            negmb[:, sl:sl + 1], negm[:, sl:sl + 1]), sem="VE",
                            waits=[("VE", v2), ("PE", state["negmb_rel"][sl])])
                        j = i % 2
                        t2 = p.op("tensor", lambda e, sl=sl, j=j: e.matmul(
                            p_z[:, j * 512:j * 512 + 1], c_on_rb[:, :],
                            negmb[:, sl:sl + 1],
                            start=True, stop=True), sem="PE",
                            waits=[("VE", v3), ("SE", state["pbc_rel"]),
                                   state["pz_rel"][j]])
                        state["negmb_rel"][sl] = t2
                        state["biasM_ready"][sl] = p.op("scalar",
                            lambda e, sl=sl, j=j: e.activation(
                                biasM[:, sl:sl + 1], p_z[:, j * 512:j * 512 + 1],
                                AF.Copy),
                            sem="SE", waits=[("PE", t2)])
                        state["pbc_rel"] = state["biasM_ready"][sl]
                        state["pz_rel"][j] = ("SE", state["biasM_ready"][sl])

                    return units, finalize

                def passB_units(i):
                    s, qc = divmod(i, NQC)
                    nkt = (qc + 1) * 4
                    qoff = s * SEQ + qc * 512
                    zsl = i % 2
                    units = []

                    def st_mm(kt):
                        sl = kt % 3
                        wts = [("SE", state["st_rel"][sl])]
                        return p.op("tensor", lambda e, kt=kt, sl=sl: e.matmul(
                            p_st[:, sl * 512:(sl + 1) * 512],
                            k_sb[:, s * SEQ + kt * 128:s * SEQ + kt * 128 + 128],
                            q_sb[:, qoff:qoff + 512], start=True, stop=True),
                            sem="PE", waits=wts)

                    pe_st = {}

                    def unit(kt, nkt=nkt, s=s, qc=qc, i=i):
                        if kt == 0:
                            pe_st[0] = st_mm(0)
                            if nkt > 1:
                                pe_st[1] = st_mm(1)
                        sl = kt % 3
                        ps = kt % 4
                        se = p.op("scalar", lambda e, kt=kt, sl=sl, ps=ps, i=i:
                            e.activation(pt_sb[:, ps * 512:(ps + 1) * 512],
                                         p_st[:, sl * 512:(sl + 1) * 512], AF.Exp,
                                         bias=biasM[:, i % 2:i % 2 + 1]),
                            sem="SE", waits=[("PE", pe_st[kt]),
                                             ("PE", state["pt_rel"][ps]),
                                             ("SE", state["biasM_ready"][i % 2])])
                        state["st_rel"][sl] = se
                        ready = ("SE", se)
                        i_d = kt - qc * 4
                        if i_d >= 0:
                            dv = p.op("vector", lambda e, ps=ps, i_d=i_d: e.tensor_tensor(
                                pt_sb[:, ps * 512:(ps + 1) * 512],
                                pt_sb[:, ps * 512:(ps + 1) * 512],
                                c_mask_s[:, i_d * 512:(i_d + 1) * 512], ALU.mult),
                                sem="VE", waits=[("SE", se), ("MASK", state["mask_ok"])])
                            ready = ("VE", dv)
                        if kt + 2 < nkt:
                            pe_st[kt + 2] = st_mm(kt + 2)
                        wts = [ready]
                        if kt == 0:
                            wts.append(("VE", state["prow_rel"][zsl]))
                        p.op("tensor", lambda e, kt=kt, ps=ps, nkt=nkt, zsl=zsl: e.matmul(
                            p_row[32 * zsl:32 * zsl + 1, :], c_on_c[:, :],
                            pt_sb[:, ps * 512:(ps + 1) * 512],
                            start=(kt == 0), stop=(kt == nkt - 1)),
                            sem=None, waits=wts)
                        wts2 = []
                        if kt == 0:
                            wts2.append(state["pz_rel"][zsl])
                        state["pt_rel"][ps] = p.op("tensor",
                            lambda e, kt=kt, ps=ps, s=s, nkt=nkt, zsl=zsl: e.matmul(
                                p_z[:, zsl * 512:(zsl + 1) * 512],
                                vt_sb[:, (s * 64 + kt) * 128:(s * 64 + kt) * 128 + 128],
                                pt_sb[:, ps * 512:(ps + 1) * 512],
                                start=(kt == 0), stop=(kt == nkt - 1)),
                            sem="PE", waits=wts2)

                    for kt in range(nkt):
                        units.append(lambda kt=kt: unit(kt))

                    def finalize(i=i, s=s, qc=qc, qoff=qoff, zsl=None):
                        zsl = i % 2
                        pe_last = p.val("PE")
                        v1 = p.op("vector", lambda e, zsl=zsl: e.tensor_copy(
                            sumr[:, :], p_row[32 * zsl:32 * zsl + 1, :]),
                            sem="VE", waits=[("PE", pe_last)])
                        state["prow_rel"][zsl] = v1
                        if sums_hist is not None and i in sums_slots:
                            sslot = sums_slots[i]
                            p.op("vector", lambda e, sslot=sslot: e.tensor_copy(
                                sums_hist[:, sslot * 512:(sslot + 1) * 512], sumr[:, :]),
                                sem="VE", waits=[("VE", v1)])
                        v2 = p.op("vector", lambda e: e.reciprocal(rsum[:, :], sumr[:, :]),
                                  sem="VE", waits=[("VE", v1)])
                        j = (i + 1) % 2
                        t1 = p.op("tensor", lambda e, j=j: e.matmul(
                            p_z[:, j * 512:(j + 1) * 512], c_on_rf[:, :], rsum[:, :],
                            start=True, stop=True),
                            sem="PE", waits=[("VE", v2), ("SE", state["pbc_rel"]),
                                             state["pz_rel"][j]])
                        s1 = p.op("scalar", lambda e, j=j: e.activation(
                            bcsum[:, :], p_z[:, j * 512:(j + 1) * 512], AF.Copy),
                            sem="SE", waits=[("PE", t1)])
                        state["pbc_rel"] = s1
                        state["pz_rel"][j] = ("SE", s1)
                        v3 = p.op("vector", lambda e, qoff=qoff, zsl=zsl: e.tensor_tensor(
                            z_sb[:, qoff:qoff + 512], p_z[:, zsl * 512:(zsl + 1) * 512],
                            bcsum[:, :], ALU.mult),
                            sem="VE", waits=[("SE", s1)])
                        state["pz_rel"][zsl] = ("VE", v3)

                    return units, finalize

                npair = HPC * NQC
                prevB = None
                pendingA_fin = None
                ua, fa = passA_units(0)
                for u in ua:
                    u()
                fa()
                for i in range(npair):
                    ub, fb = passB_units(i)
                    if i + 1 < npair:
                        ua, fa = passA_units(i + 1)
                    else:
                        ua, fa = [], None
                    if interleave:
                        nb, na = len(ub), len(ua)
                        ai = 0
                        for j, u in enumerate(ub):
                            u()
                            while ai < na and ai * nb < (j + 1) * na:
                                ua[ai]()
                                ai += 1
                            if ai == na and fa is not None:
                                fa()
                                fa = None
                        while ai < na:
                            ua[ai]()
                            ai += 1
                        if fa is not None:
                            fa()
                        fb()
                    else:
                        for u in ub:
                            u()
                        fb()
                        for u in ua:
                            u()
                        if fa is not None:
                            fa()

                if "t_z" in taps:
                    p.op("sync", lambda e: e.dma_start(out=tap_d["t_z"], in_=z_sb[:, :]),
                         sem="TAP", n=16, waits=[("VE", p.val("VE"))])
                if sums_hist is not None:
                    p.op("sync", lambda e: e.dma_start(out=tap_d["t_sums"], in_=sums_hist[:, :]),
                         sem="TAP", n=16, waits=[("VE", p.val("VE"))])
                p.flush(block)

            bar_attn_end = [(s, p.val(s)) for s in ("PE", "SE", "VE")]

            if level >= 3:
              with ExitStack() as wo_ctx:
                WE = wo_ctx.enter_context
                wo_sb = WE(nc.sbuf_tensor("wo_sb", [128, HPC * D], BF16))
                rs_stage = WE(nc.sbuf_tensor("rs_stage", [128, 2 * D], BF16))
                p_at = WE(nc.psum_tensor("p_at", [128, 2 * D], F32))
                p.barrier(["gpsimd", "tensor", "scalar", "sync"])
                p.op("gpsimd", lambda e: e.dma_start(out=wo_sb[:, :], in_=wo),
                     sem="WOL", n=16)
                GLD_WO = p.val("WOL")
                at_rel = [0, 0]
                stg_rel = [0, 0]
                for gt in range(SEQ // 128):
                    g = gt % 2
                    for nn in range(4):
                        for s in range(HPC):
                            wts = []
                            if nn == 0 and s == 0:
                                wts = [("SE", at_rel[g]), ("WOL", GLD_WO)]
                            p.op("tensor", lambda e, gt=gt, g=g, nn=nn, s=s: e.matmul(
                                p_at[:, g * D + nn * 512:g * D + nn * 512 + 512],
                                z_sb[:, s * SEQ + gt * 128:s * SEQ + gt * 128 + 128],
                                wo_sb[:, s * D + nn * 512:s * D + nn * 512 + 512],
                                start=(s == 0), stop=(s == HPC - 1)),
                                sem="PE" if (nn == 3 and s == HPC - 1) else None,
                                waits=wts)
                    pe_gt = p.val("PE")
                    at_rel[g] = p.op("scalar", lambda e, g=g: e.activation(
                        rs_stage[:, g * D:(g + 1) * D], p_at[:, g * D:(g + 1) * D],
                        AF.Copy), sem="SE",
                        waits=[("PE", pe_gt), (f"SR{g}", stg_rel[g])])
                    stg_rel[g] = p.op("sync", lambda e, gt=gt, g=g: e.dma_start(
                        out=rs_ctb[gt * 128:(gt + 1) * 128, :],
                        in_=rs_stage[:, g * D:(g + 1) * D]), sem=f"SR{g}", n=16,
                        waits=[("SE", at_rel[g])])
                SR0_ALL, SR1_ALL = p.val("SR0"), p.val("SR1")
                p.op("gpsimd", lambda e: e.collective_compute(
                    "ReduceScatter", ALU.add, replica_groups=[list(range(NCORE))],
                    ins=[rs_ctb.opt()], outs=[rs_sc.opt()]), sem="CCR", n=1,
                    waits=[("SR0", SR0_ALL), ("SR1", SR1_ALL)])
                CC_RS = p.val("CCR")
                if "t_rs" in taps:
                    p.op("sync", lambda e: e.dma_start(out=tap_d["t_rs"], in_=rs_sc),
                         sem="TAP", n=16, waits=[("CCR", CC_RS)])
                p.flush(block)
            z_stack.close()

        if level >= 4:
            emit_ffn([1], add_rs=True, bar=bar_attn_end)

        if level >= 4:
            with ExitStack() as fin_ctx:
                GE = fin_ctx.enter_context
                fpart = GE(nc.sbuf_tensor("fpart", [128, 2 * D], F32))
                frs = GE(nc.sbuf_tensor("frs", [128, 2 * D], BF16))
                fout = GE(nc.sbuf_tensor("fout", [128, 2 * D], F32))
                p.barrier(["sync", "vector"])
                fp_rel = [0, 0]
                fo_rel = [0, 0]
                for tt in range(NTT // 2):
                    sl = tt % 2
                    lp = p.op("sync", lambda e, tt=tt, sl=sl: e.dma_start(
                        out=fpart[:, sl * D:(sl + 1) * D],
                        in_=part_d[tt * 128:(tt + 1) * 128, :]), sem=f"FP{sl}", n=16,
                        waits=[("VE", fp_rel[sl])] + ([("SP0", p.val("SP0")),
                                                       ("SP1", p.val("SP1"))] if tt < 2 else []))
                    lr = p.op("sync", lambda e, tt=tt, sl=sl: e.dma_start(
                        out=frs[:, sl * D:(sl + 1) * D],
                        in_=rs_sc[tt * 128:(tt + 1) * 128, :]), sem=f"FR{sl}", n=16,
                        waits=[("CCR", CC_RS), ("VE", fp_rel[sl])])
                    v = p.op("vector", lambda e, sl=sl: e.tensor_tensor(
                        fout[:, sl * D:(sl + 1) * D], fpart[:, sl * D:(sl + 1) * D],
                        frs[:, sl * D:(sl + 1) * D], ALU.add), sem="VE",
                        waits=[(f"FP{sl}", lp), (f"FR{sl}", lr),
                               (f"SO{sl}", fo_rel[sl])])
                    fp_rel[sl] = v
                    fo_rel[sl] = p.op("sync", lambda e, tt=tt, sl=sl: e.dma_start(
                        out=out_d[tt * 128:(tt + 1) * 128, :],
                        in_=fout[:, sl * D:(sl + 1) * D]), sem=f"SO{sl}", n=16,
                        waits=[("VE", v)])
                if "t_rs" in taps:
                    p.op("sync", lambda e: e.dma_start(out=tap_d["t_rs"], in_=rs_sc),
                         sem="TAP", n=16, waits=[("CCR", CC_RS)])
                p.flush(block)
            z_stack.close()

        for s in ("STA", "SR0", "SR1", "SP0", "SP1", "SO0", "SO1", "TAP"):
            if s in p.cnt:
                p.wait("sync", s, p.val(s))
        p.flush(block)

    return nc



def _prep_inputs(L0_attn_x, L0_attn_W_Q, L0_attn_W_K, L0_attn_W_V, L0_attn_W_O,
                 L0_attn_b_Q, L0_attn_b_K, L0_attn_b_V, L0_attn_b_O,
                 L0_ffn_w_gate, L0_ffn_w_up, L0_ffn_w_down,
                 L0_ffn_s_gate, L0_ffn_s_up, L0_ffn_s_down):
    x = np.asarray(L0_attn_x, np.float32).reshape(SEQ, D)
    WQ = np.asarray(L0_attn_W_Q, np.float32)
    WK = np.asarray(L0_attn_W_K, np.float32)
    WV = np.asarray(L0_attn_W_V, np.float32)
    WO = np.asarray(L0_attn_W_O, np.float32)
    bQ = np.asarray(L0_attn_b_Q, np.float32)
    bK = np.asarray(L0_attn_b_K, np.float32)
    bV = np.asarray(L0_attn_b_V, np.float32)
    bO = np.asarray(L0_attn_b_O, np.float32).reshape(1, D)
    wg = np.asarray(L0_ffn_w_gate, np.int8)
    wu = np.asarray(L0_ffn_w_up, np.int8)
    wd = np.asarray(L0_ffn_w_down, np.int8)
    sgv = np.asarray(L0_ffn_s_gate, np.float32)
    suv = np.asarray(L0_ffn_s_up, np.float32)
    sd = np.asarray(L0_ffn_s_down, np.float32).reshape(1, D)

    consts = dict(
        c_id=np.eye(128, dtype=ml_dtypes.bfloat16),
        c_idf=np.eye(128, dtype=np.float32),
        c_on_c=np.ones((128, 1), ml_dtypes.bfloat16),
        c_on_rb=np.ones((1, 128), ml_dtypes.bfloat16),
        c_on_rf=np.ones((1, 128), np.float32),
        sg_i=np.ascontiguousarray(sgv.reshape(64, 128).T),
        su_i=np.ascontiguousarray(suv.reshape(64, 128).T),
        bo_r=bO, sd_r=sd, wg_d=wg, wu_d=wu, wd_d=wd,
    )
    mask = np.zeros((128, 4 * 512), ml_dtypes.bfloat16)
    row = np.arange(128)[:, None]
    col = np.arange(512)[None, :]
    for i in range(4):
        mask[:, i * 512:(i + 1) * 512] = (col >= i * 128 + row).astype(
            ml_dtypes.bfloat16)
    consts["c_mask"] = mask

    in_maps = []
    for c in range(NCORE):
        hs = slice(HPC * c, HPC * (c + 1))
        m = dict(consts)
        m["x_sh"] = np.ascontiguousarray(x[c * TOK:(c + 1) * TOK])
        m["wq"] = np.ascontiguousarray(
            WQ[hs].reshape(HPC, KC, 128, 128).transpose(2, 0, 1, 3).reshape(128, -1))
        m["wk"] = np.ascontiguousarray(
            WK[hs].reshape(HPC, KC, 128, 128).transpose(2, 0, 1, 3).reshape(128, -1))
        m["wv"] = np.ascontiguousarray(
            WV[hs].reshape(HPC, KC, 128, 128).transpose(2, 0, 1, 3).reshape(128, -1))
        m["wo"] = np.ascontiguousarray(
            WO[hs].transpose(1, 0, 2).reshape(128, -1))
        m["bq"] = np.ascontiguousarray(bQ[hs].T)
        m["bk"] = np.ascontiguousarray(bK[hs].T)
        m["bv"] = np.ascontiguousarray(bV[hs].T)
        in_maps.append(m)
    return in_maps


_NC_CACHE = {}


def kernel(**inputs):
    in_maps = _prep_inputs(**inputs)
    key = "main"
    if key not in _NC_CACHE:
        _NC_CACHE[key] = build()
    nc = _NC_CACHE[key]
    res = run_bass_kernel_spmd(nc, in_maps, core_ids=list(range(NCORE)))
    out = np.concatenate([np.asarray(r["out"]) for r in res.results], axis=0)
    return out.reshape(1, SEQ, D)


# version 19
# speedup vs baseline: 1.0113x; 1.0113x over previous
import math
import numpy as np
import ml_dtypes
from contextlib import ExitStack

import concourse.bass as bass
import concourse.mybir as mybir
from concourse.bass_utils import run_bass_kernel_spmd

F32 = mybir.dt.float32
BF16 = mybir.dt.bfloat16
I8 = mybir.dt.int8
AF = mybir.ActivationFunctionType
ALU = mybir.AluOpType
AX = mybir.AxisListType.X

SEQ = 8192
D = 2048
NH = 16
DH = 128
INTER = 8192
NCORE = 8
TOK = SEQ // NCORE
HPC = NH // NCORE
KC = D // 128
NQC = SEQ // 512
ISCALE = 1.0 / math.sqrt(DH)
TC = 512
NTT = TOK // 128


class Plan:

    ENGS = ("sync", "tensor", "vector", "scalar", "gpsimd")

    def __init__(self, nc, ctx):
        self.q = {e: [] for e in self.ENGS}
        self.cnt = {}
        self.handles = {}
        self._nc = nc
        self._ctx = ctx

    def sem(self, name):
        if name not in self.handles:
            self.handles[name] = self._ctx.enter_context(
                self._nc.semaphore("sem" + name))
            self.cnt[name] = 0
        return name

    def inc(self, sem, n=1):
        self.sem(sem)
        self.cnt[sem] += n
        return self.cnt[sem]

    def val(self, sem):
        self.sem(sem)
        return self.cnt[sem]

    def emit(self, eng, fn):
        self.q[eng].append(fn)

    def op(self, eng, make, sem=None, n=1, waits=()):
        target = self.inc(sem, n) if sem else None
        waits = tuple(waits)
        for ws, wv in waits:
            self.sem(ws)

        def fn(e, make=make, sem=sem, n=n, waits=waits):
            for ws, wv in waits:
                if wv > 0:
                    e.wait_ge(self.handles[ws], wv)
            ins = make(e)
            if sem is not None:
                ins.then_inc(self.handles[sem], n)

        self.emit(eng, fn)
        return target

    def wait(self, eng, sem, value):
        self.sem(sem)
        if value <= 0:
            return

        def fn(e, sem=sem, value=value):
            e.wait_ge(self.handles[sem], value)

        self.emit(eng, fn)

    def barrier(self, engs):
        vals = [(s, self.val(s)) for s in ("PE", "SE", "VE")]
        for eng in engs:
            for s, v in vals:
                self.wait(eng, s, v)

    def flush(self, block):
        for eng_name in self.ENGS:
            q = self.q[eng_name]
            if not q:
                continue
            getattr(block, eng_name)(lambda e, q=tuple(q): [f(e) for f in q])
            self.q[eng_name] = []


def build(taps=(), level=4, interleave=True):
    taps = set(taps)
    nc = bass.Bass("TRN2", target_bir_lowering=False, debug=False)

    x_sh = nc.dram_tensor("x_sh", [TOK, D], F32, kind="ExternalInput").ap()
    wq = nc.dram_tensor("wq", [128, HPC * KC * 128], F32, kind="ExternalInput").ap()
    wk = nc.dram_tensor("wk", [128, HPC * KC * 128], F32, kind="ExternalInput").ap()
    wv = nc.dram_tensor("wv", [128, HPC * KC * 128], F32, kind="ExternalInput").ap()
    wo = nc.dram_tensor("wo", [128, HPC * D], F32, kind="ExternalInput").ap()
    bq = nc.dram_tensor("bq", [128, HPC], F32, kind="ExternalInput").ap()
    bk = nc.dram_tensor("bk", [128, HPC], F32, kind="ExternalInput").ap()
    bv = nc.dram_tensor("bv", [128, HPC], F32, kind="ExternalInput").ap()
    bo_r = nc.dram_tensor("bo_r", [1, D], F32, kind="ExternalInput").ap()
    sg_i = nc.dram_tensor("sg_i", [128, 64], F32, kind="ExternalInput").ap()
    su_i = nc.dram_tensor("su_i", [128, 64], F32, kind="ExternalInput").ap()
    sd_r = nc.dram_tensor("sd_r", [1, D], F32, kind="ExternalInput").ap()
    wg_d = nc.dram_tensor("wg_d", [D, INTER], I8, kind="ExternalInput").ap()
    wu_d = nc.dram_tensor("wu_d", [D, INTER], I8, kind="ExternalInput").ap()
    wd_d = nc.dram_tensor("wd_d", [INTER, D], I8, kind="ExternalInput").ap()
    c_id_d = nc.dram_tensor("c_id", [128, 128], BF16, kind="ExternalInput").ap()
    c_idf_d = nc.dram_tensor("c_idf", [128, 128], F32, kind="ExternalInput").ap()
    c_on_c_d = nc.dram_tensor("c_on_c", [128, 1], BF16, kind="ExternalInput").ap()
    c_on_rb_d = nc.dram_tensor("c_on_rb", [1, 128], BF16, kind="ExternalInput").ap()
    c_on_rf_d = nc.dram_tensor("c_on_rf", [1, 128], F32, kind="ExternalInput").ap()
    c_mask_d = nc.dram_tensor("c_mask", [128, 4 * 512], BF16, kind="ExternalInput").ap()

    out_d = nc.dram_tensor("out", [TOK, D], F32, kind="ExternalOutput").ap()
    tap_d = {}
    for t in taps:
        shape = {"t_xn": [128, KC * TOK], "t_q": [128, HPC * SEQ],
                 "t_k": [128, HPC * SEQ], "t_vt": [128, HPC * SEQ],
                 "t_z": [128, HPC * SEQ], "t_rs": [TOK, D],
                 "t_sums": [1, 4 * 512], "t_dbg": [128, 2 * D],
                 "t_fuse": [128, (INTER // 128) * TC]}[t]
        dt = F32 if t == "t_sums" else BF16
        tap_d[t] = nc.dram_tensor(t, shape, dt, kind="ExternalOutput").ap()

    xn_ctb = nc.dram_tensor("xn_ctb", [D, TOK], BF16).ap()
    xn_all = nc.dram_tensor("xn_all", [NCORE * D, TOK], BF16).ap()
    rs_ctb = nc.dram_tensor("rs_ctb", [SEQ, D], BF16).ap()
    rs_sc = nc.dram_tensor("rs_sc", [TOK, D], BF16).ap()
    part_d = nc.dram_tensor("part_d", [TOK, D], F32).ap()
    bc_d = nc.dram_tensor("bc_d", [128, 2 * D], F32).ap()

    with ExitStack() as top:
        E = top.enter_context
        p = Plan(nc, top)
        block = E(nc.Block())

        c_id = E(nc.sbuf_tensor("c_id_s", [128, 128], BF16))
        c_idf = E(nc.sbuf_tensor("c_idf_s", [128, 128], F32))
        c_on_c = E(nc.sbuf_tensor("c_on_c_s", [128, 1], BF16))
        c_on_rb = E(nc.sbuf_tensor("c_on_rb_s", [1, 128], BF16))
        c_on_rf = E(nc.sbuf_tensor("c_on_rf_s", [1, 128], F32))
        c_bq = E(nc.sbuf_tensor("c_bq", [128, HPC], F32))
        c_bk = E(nc.sbuf_tensor("c_bk", [128, HPC], F32))
        c_bv = E(nc.sbuf_tensor("c_bv", [128, HPC], F32))
        c_bqs = E(nc.sbuf_tensor("c_bqs", [128, HPC], F32))
        c_sg = E(nc.sbuf_tensor("c_sg", [128, 64], F32))
        c_su = E(nc.sbuf_tensor("c_su", [128, 64], F32))
        xn_mine = E(nc.sbuf_tensor("xn_mine", [128, KC * TOK], BF16))

        for dst, src in ((c_id, c_id_d), (c_idf, c_idf_d), (c_on_c, c_on_c_d), (c_on_rb, c_on_rb_d),
                         (c_on_rf, c_on_rf_d), (c_bq, bq), (c_bk, bk), (c_bv, bv),
                         (c_sg, sg_i), (c_su, su_i)):
            p.op("sync", lambda e, d=dst, s=src: e.dma_start(out=d[:, :], in_=s),
                 sem="C", n=16)
        LD_CONST = p.val("C")

        with ExitStack() as ph0_ctx:
            rows = ph0_ctx.enter_context(nc.sbuf_tensor("ph0_rows", [1, 2 * D], F32))
            bc_tmp = ph0_ctx.enter_context(nc.sbuf_tensor("bc_tmp", [128, 2 * D], F32))
            p_tmp = ph0_ctx.enter_context(nc.psum_tensor("p_tmp", [128, D], F32))
            c_bo_bc = bc_tmp[:, 0:D]
            c_sd_bc = bc_tmp[:, D:2 * D]
            p.op("sync", lambda e: e.dma_start(out=rows[:, 0:D], in_=bo_r), sem="C", n=16)
            p.op("sync", lambda e: e.dma_start(out=rows[:, D:2 * D], in_=sd_r), sem="C", n=16)
            C_ALL = p.val("C")
            p.op("vector", lambda e: e.tensor_scalar_mul(c_bqs[:, :], c_bq[:, :], ISCALE),
                 sem="VE", waits=[("C", C_ALL)])
            for j in range(4):
                p.op("tensor", lambda e, j=j: e.matmul(
                    p_tmp[:, j * 512:(j + 1) * 512], c_on_rf[:, :],
                    rows[:, j * 512:(j + 1) * 512], start=True, stop=True),
                    sem="PE", waits=[("C", C_ALL)] if j == 0 else ())
            PE_BO = p.val("PE")
            p.op("scalar", lambda e: e.activation(c_bo_bc, p_tmp[:, :], AF.Copy),
                 sem="SE", waits=[("PE", PE_BO)])
            SE_BO = p.val("SE")
            for j in range(4):
                p.op("tensor", lambda e, j=j: e.matmul(
                    p_tmp[:, j * 512:(j + 1) * 512], c_on_rf[:, :],
                    rows[:, D + j * 512:D + (j + 1) * 512], start=True, stop=True),
                    sem="PE", waits=[("SE", SE_BO)] if j == 0 else ())
            PE_SD = p.val("PE")
            p.op("scalar", lambda e: e.activation(c_sd_bc, p_tmp[:, :], AF.Copy),
                 sem="SE", waits=[("PE", PE_SD)])
            SE_SD = p.val("SE")
            st_bc = p.op("sync", lambda e: e.dma_start(out=bc_d, in_=bc_tmp[:, :]),
                         sem="BCD", n=16, waits=[("SE", SE_SD)])

            with ExitStack() as ln_ctx:
                LE = ln_ctx.enter_context
                xs = LE(nc.sbuf_tensor("xs", [128, 2 * D], F32))
                xn_tm = LE(nc.sbuf_tensor("xn_tm", [128, D], BF16))
                sqj = LE(nc.sbuf_tensor("sqj", [128, D], BF16))
                ln_sc = LE(nc.sbuf_tensor("ln_sc", [128, 8], F32))
                p_xt = LE(nc.psum_tensor("p_xt", [128, D], BF16))
                xs_rel = [0, 0]
                tm_rel = 0
                pxt_rel = 0
                for xt in range(NTT):
                    sl = xt % 2
                    ld = p.op("sync", lambda e, xt=xt, sl=sl: e.dma_start(
                        out=xs[:, sl * D:(sl + 1) * D],
                        in_=x_sh[xt * 128:(xt + 1) * 128, :]), sem=f"X{sl}", n=16,
                        waits=[("SE", xs_rel[sl])])
                    xv = lambda: xs[:, sl * D:(sl + 1) * D]
                    v1 = p.op("vector", lambda e, sl=sl: e.tensor_reduce(
                        ln_sc[:, 0:1], xs[:, sl * D:(sl + 1) * D], AX, ALU.add),
                        sem="VE", waits=[(f"X{sl}", ld)])
                    s1 = p.op("scalar", lambda e, sl=sl: e.activation(
                        sqj[:, :], xs[:, sl * D:(sl + 1) * D], AF.Square,
                        accum_out=ln_sc[:, 2:3]), sem="SE",
                        waits=[(f"X{sl}", ld), ("VE", pxt_rel)])
                    v2 = p.op("vector", lambda e: e.tensor_scalar_mul(
                        ln_sc[:, 1:2], ln_sc[:, 0:1], 1.0 / D), sem="VE",
                        waits=[("VE", v1)])
                    v3 = p.op("vector", lambda e: e.tensor_scalar_mul(
                        ln_sc[:, 3:4], ln_sc[:, 2:3], 1.0 / D), sem="VE",
                        waits=[("SE", s1), ("VE", v2)])
                    v4 = p.op("vector", lambda e: e.tensor_tensor(
                        ln_sc[:, 4:5], ln_sc[:, 1:2], ln_sc[:, 1:2], ALU.mult),
                        sem="VE", waits=[("VE", v3)])
                    v5 = p.op("vector", lambda e: e.tensor_tensor(
                        ln_sc[:, 3:4], ln_sc[:, 3:4], ln_sc[:, 4:5], ALU.subtract),
                        sem="VE", waits=[("VE", v4)])
                    s2 = p.op("scalar", lambda e: e.activation(
                        ln_sc[:, 4:5], ln_sc[:, 3:4], AF.Ln, bias=np.float32(1e-5)), sem="SE",
                        waits=[("VE", v5)])
                    s3 = p.op("scalar", lambda e: e.activation(
                        ln_sc[:, 5:6], ln_sc[:, 4:5], AF.Exp, scale=np.float32(-0.5)), sem="SE",
                        waits=[("SE", s2)])
                    v6 = p.op("vector", lambda e: e.tensor_tensor(
                        ln_sc[:, 6:7], ln_sc[:, 1:2], ln_sc[:, 5:6], ALU.mult),
                        sem="VE", waits=[("SE", s3), ("VE", v5)])
                    v7 = p.op("vector", lambda e: e.tensor_scalar_mul(
                        ln_sc[:, 7:8], ln_sc[:, 6:7], -1.0), sem="VE",
                        waits=[("VE", v6)])
                    s4 = p.op("scalar", lambda e, sl=sl: e.activation(
                        xn_tm[:, :], xs[:, sl * D:(sl + 1) * D], AF.Identity,
                        bias=ln_sc[:, 7:8], scale=ln_sc[:, 5:6]), sem="SE",
                        waits=[("VE", v7), ("PE", tm_rel)])
                    xs_rel[sl] = s4
                    for k in range(KC):
                        p.op("tensor", lambda e, k=k: e.transpose(
                            p_xt[:, k * 128:(k + 1) * 128],
                            xn_tm[:, k * 128:(k + 1) * 128], c_id[:, :]),
                            sem="PE",
                            waits=([("SE", s4), ("VE", pxt_rel)] if k == 0 else ()))
                    tm_rel = p.val("PE")
                    pxt_rel = p.op("vector", lambda e, xt=xt: e.tensor_copy(
                        xn_mine.ap().rearrange("p (k t) -> p k t", k=KC)
                        [:, :, xt * 128:(xt + 1) * 128],
                        p_xt.ap().rearrange("p (k t) -> p k t", k=KC)),
                        sem="VE", waits=[("PE", tm_rel)])
                VE_XN = p.val("VE")
                st1 = p.op("sync", lambda e: e.dma_start(
                    out=xn_ctb.rearrange("(k p) t -> p k t", p=128),
                    in_=xn_mine.ap().rearrange("p (k t) -> p k t", k=KC)),
                    sem="STA", n=16, waits=[("VE", VE_XN)])
                if "t_xn" in taps:
                    p.op("sync", lambda e: e.dma_start(out=tap_d["t_xn"], in_=xn_mine[:, :]),
                         sem="TAP", n=16, waits=[("VE", VE_XN)])
                p.flush(block)

        def plan_ag():
            p.op("gpsimd", lambda e: e.collective_compute(
                "AllGather", ALU.bypass, replica_groups=[list(range(NCORE))],
                ins=[xn_ctb.opt()], outs=[xn_all.opt()]), sem="CCA", n=1,
                waits=[("STA", st1)])

        if level >= 1 and level < 4:
            plan_ag()

        def emit_ffn(tcs, add_rs=False, bar=None, ag_hook=None):
            sfx = str(tcs[0])
            hookbox = [ag_hook]
            with ExitStack() as ffn_ctx:
              FE = ffn_ctx.enter_context
              wstream = FE(nc.sbuf_tensor("wstream" + sfx, [128, 3 * 8192], BF16))
              xld = FE(nc.sbuf_tensor("xld" + sfx, [128, 2 * D], F32))
              bcs = FE(nc.sbuf_tensor("bcs" + sfx, [128, 2 * D], F32))
              rs_t = FE(nc.sbuf_tensor("rs_t" + sfx, [128, D], BF16)) if add_rs else None
              fuse = FE(nc.sbuf_tensor("fuse" + sfx, [128, (INTER // 128) * TC], BF16))
              u_sb = FE(nc.sbuf_tensor("u_sb" + sfx, [128, 8 * TC], BF16))
              out_sb = FE(nc.sbuf_tensor("out_sb" + sfx, [128, 2 * D], F32))

              if bar is None:
                  p.barrier(["gpsimd", "sync", "tensor", "scalar", "vector"])
              else:
                  for eng in ("gpsimd", "sync"):
                      for s, v in bar:
                          p.wait(eng, s, v)
                  p.barrier(["tensor", "scalar", "vector"])
              for eng in ("scalar", "vector"):
                  for s in ("SP0", "SP1", "SR0", "SR1", "TAP"):
                      p.wait(eng, s, p.val(s))
              ldbc = p.op("sync", lambda e: e.dma_start(out=bcs[:, :], in_=bc_d),
                          sem=f"BCL{sfx}", n=16, waits=[("BCD", p.val("BCD"))])
              wst_rel = [0, 0, 0]
              pf_rel = [0, 0]
              gu_rel = 0
              fuse_rel = 0
              xld_rel = [0, 0]
              rst_rel = [0]
              out_rel = [0, 0]
              wslot = 0
              for tc in tcs:
                  with nc.psum_tensor(f"p_g{tc}", [128, 8 * 512], F32) as p_g:
                      ev_rel = [0] * 8
                      for sl in range(8):
                          for wi, (wdr, sc_ap, dst) in enumerate(
                                  ((wg_d, c_sg, None), (wu_d, c_su, u_sb))):
                              lds = []
                              for kh in range(2):
                                  ws = wslot
                                  wslot = (wslot + 1) % 3
                                  lds.append((ws, p.op("gpsimd",
                                      lambda e, wdr=wdr, kh=kh, sl=sl, ws=ws: e.dma_start(
                                          out=wstream.ap().rearrange(
                                              "p (b k f) -> p b k f", b=3, k=8)
                                          [:, ws, :, :],
                                          in_=wdr.rearrange("(kc p) f -> p kc f", p=128)
                                          [:, kh * 8:(kh + 1) * 8, sl * 1024:(sl + 1) * 1024]),
                                      sem=f"W{ws}", n=16,
                                      waits=([("PE", wst_rel[ws])] if wst_rel[ws]
                                             else [("BCD", p.val("BCD"))]))))
                              if hookbox[0] is not None:
                                  hookbox[0]()
                                  hookbox[0] = None
                              for kh in range(2):
                                  ws, ld = lds[kh]
                                  for m in range(8):
                                      for k in range(8):
                                          wts = []
                                          if m == 0 and k == 0:
                                              wts.append((f"W{ws}", ld))
                                          if k == 0 and m == 0 and kh == 0:
                                              if wi == 0:
                                                  wts.append(("VE", gu_rel))
                                                  if sl == 0:
                                                      wts.append(("VE", pf_rel[0]))
                                                      wts.append(("VE", pf_rel[1]))
                                          if k == 0 and kh == 0 and wi == 1:
                                              wts.append(("SE", ev_rel[m]))
                                          last_half = (m == 7 and k == 7)
                                          last_bank = (kh == 1 and k == 7)
                                          p.op("tensor",
                                              lambda e, ws=ws, k=k, m=m, kh=kh, tc=tc:
                                              e.matmul(
                                                  p_g[:, m * 512:(m + 1) * 512],
                                                  wstream.ap().rearrange(
                                                      "p (b k f) -> p b k f", b=3, k=8)
                                                  [:, ws, k, m * 128:(m + 1) * 128],
                                                  xn_mine[:, (kh * 8 + k) * 1024 + tc * TC:
                                                          (kh * 8 + k) * 1024 + tc * TC + TC],
                                                  start=(kh == 0 and k == 0),
                                                  stop=(kh == 1 and k == 7)),
                                              sem="PE" if (last_half or last_bank) else None,
                                              waits=wts)
                                      if kh == 1:
                                          pe_m = p.val("PE")
                                          if wi == 0:
                                              ev_rel[m] = p.op("scalar",
                                                  lambda e, m=m, sl=sl: e.activation(
                                                      fuse[:, (sl * 8 + m) * TC:
                                                           (sl * 8 + m + 1) * TC],
                                                      p_g[:, m * 512:(m + 1) * 512], AF.Silu,
                                                      scale=c_sg[:, sl * 8 + m:sl * 8 + m + 1]),
                                                  sem="SE",
                                                  waits=[("PE", pe_m)] + ([("VE", gu_rel)] if m == 0 else []))
                                          else:
                                              ev_rel[m] = p.op("scalar",
                                                  lambda e, m=m, sl=sl: e.activation(
                                                      u_sb[:, m * TC:(m + 1) * TC],
                                                      p_g[:, m * 512:(m + 1) * 512], AF.Identity,
                                                      scale=c_su[:, sl * 8 + m:sl * 8 + m + 1]),
                                                  sem="SE", waits=[("PE", pe_m)])
                                  wst_rel[ws] = p.val("PE")
                              pe_gu = p.val("PE")
                          gu_rel = p.op("vector", lambda e, sl=sl: e.tensor_tensor(
                              fuse[:, sl * 8 * TC:(sl + 1) * 8 * TC],
                              fuse[:, sl * 8 * TC:(sl + 1) * 8 * TC],
                              u_sb[:, :], ALU.mult), sem="VE",
                              waits=[("SE", ev_rel[7]), ("PE", fuse_rel)])
                  VE_FUSE = p.val("VE")
                  if tc == 0 and "t_fuse" in taps:
                      p.op("sync", lambda e: e.dma_start(out=tap_d["t_fuse"], in_=fuse[:, :]),
                           sem="ST", n=16, waits=[("VE", VE_FUSE)])

                  with nc.psum_tensor(f"p_f{tc}", [128, 2 * D], F32) as p_f:
                      for pr in range(2):
                          tiles = (2 * pr, 2 * pr + 1)
                          ldx = {}
                          for tt in tiles:
                              slx = tt % 2
                              ldx[tt] = p.op("sync", lambda e, tc=tc, tt=tt, slx=slx:
                                  e.dma_start(out=xld[:, slx * D:(slx + 1) * D],
                                              in_=x_sh[tc * TC + tt * 128:
                                                       tc * TC + tt * 128 + 128, :]),
                                  sem=f"XL{slx}", n=16, waits=[("VE", xld_rel[slx])])
                          for mg in range(16):
                              ws = wslot
                              wslot = (wslot + 1) % 3
                              ld = p.op("gpsimd", lambda e, mg=mg, ws=ws: e.dma_start(
                                  out=wstream.ap().rearrange("p (b m f) -> p b m f",
                                                             b=3, m=4)[:, ws, :, :],
                                  in_=wd_d.rearrange("(mc p) f -> p mc f", p=128)
                                  [:, mg * 4:(mg + 1) * 4, :]), sem=f"W{ws}", n=16,
                                  waits=[("PE", wst_rel[ws])])
                              for mi in range(4):
                                  m = mg * 4 + mi
                                  for ti, tt in enumerate(tiles):
                                      for nn in range(4):
                                          wts = []
                                          if mi == 0 and ti == 0 and nn == 0:
                                              wts.append((f"W{ws}", ld))
                                              if mg == 0:
                                                  wts.append(("VE", VE_FUSE))
                                          if mi == 0 and nn == 0 and mg == 0:
                                              wts.append(("VE", pf_rel[ti]))
                                          p.op("tensor", lambda e, ws=ws, mi=mi, m=m, ti=ti,
                                              tt=tt, nn=nn, tc=tc: e.matmul(
                                                  p_f[:, ti * D + nn * 512:
                                                      ti * D + nn * 512 + 512],
                                                  fuse[:, m * TC + tt * 128:
                                                       m * TC + tt * 128 + 128],
                                                  wstream.ap().rearrange(
                                                      "p (b m f) -> p b m f", b=3, m=4)
                                                  [:, ws, mi, nn * 512:(nn + 1) * 512],
                                                  start=(m == 0), stop=(m == 63)),
                                              sem="PE" if (mi == 3 and ti == 1 and nn == 3)
                                              else None,
                                              waits=wts)
                              wst_rel[ws] = p.val("PE")
                          fuse_rel = p.val("PE")
                          pe_pair = p.val("PE")
                          for ti, tt in enumerate(tiles):
                              slx = tt % 2
                              slo = tt % 2
                              osem = f"SO{slo}" if add_rs else f"SP{slo}"
                              v1 = p.op("vector", lambda e, ti=ti, slo=slo: e.tensor_tensor(
                                  out_sb[:, slo * D:(slo + 1) * D],
                                  p_f[:, ti * D:(ti + 1) * D], bcs[:, D:2 * D], ALU.mult),
                                  sem="VE", waits=[("PE", pe_pair), (osem, out_rel[slo]),
                                                   (f"BCL{sfx}", ldbc)])
                              v2 = p.op("vector", lambda e, slo=slo: e.tensor_tensor(
                                  out_sb[:, slo * D:(slo + 1) * D],
                                  out_sb[:, slo * D:(slo + 1) * D], bcs[:, 0:D], ALU.add),
                                  sem="VE", waits=[("VE", v1)])
                              v3 = p.op("vector", lambda e, slo=slo, slx=slx: e.tensor_tensor(
                                  out_sb[:, slo * D:(slo + 1) * D],
                                  out_sb[:, slo * D:(slo + 1) * D],
                                  xld[:, slx * D:(slx + 1) * D], ALU.add),
                                  sem="VE", waits=[("VE", v2), (f"XL{slx}", ldx[tt])])
                              xld_rel[slx] = v3
                              if add_rs:
                                  ldr = p.op("sync", lambda e, tc=tc, tt=tt: e.dma_start(
                                      out=rs_t[:, :],
                                      in_=rs_sc[tc * TC + tt * 128:
                                                tc * TC + tt * 128 + 128, :]),
                                      sem="RST", n=16,
                                      waits=[("CCR", p.val("CCR")), ("VE", rst_rel[0])])
                                  v3 = p.op("vector", lambda e, slo=slo: e.tensor_tensor(
                                      out_sb[:, slo * D:(slo + 1) * D],
                                      out_sb[:, slo * D:(slo + 1) * D],
                                      rs_t[:, :], ALU.add),
                                      sem="VE", waits=[("VE", v3), ("RST", ldr)])
                                  rst_rel[0] = v3
                              pf_rel[ti] = v3
                              dsttn = out_d if add_rs else part_d
                              dsem = f"SO{slo}" if add_rs else f"SP{slo}"
                              out_rel[slo] = p.op("sync", lambda e, tc=tc, tt=tt, slo=slo,
                                  dsttn=dsttn: e.dma_start(
                                      out=dsttn[tc * TC + tt * 128:
                                                tc * TC + tt * 128 + 128, :],
                                      in_=out_sb[:, slo * D:(slo + 1) * D]),
                                  sem=dsem, n=16, waits=[("VE", v3)])



        if level >= 4:
            emit_ffn([0], ag_hook=plan_ag if level >= 1 else None)
        CC_AG = p.val("CCA")

        with ExitStack() as big_ctx:
            BE = big_ctx.enter_context
            q_sb = BE(nc.sbuf_tensor("q_sb", [128, HPC * SEQ], BF16))
            k_sb = BE(nc.sbuf_tensor("k_sb", [128, HPC * SEQ], BF16))
            vt_sb = BE(nc.sbuf_tensor("vt_sb", [128, HPC * SEQ], BF16))


            if level >= 1:
              with ExitStack() as qkv_ctx:
                QE = qkv_ctx.enter_context
                wq_sb = QE(nc.sbuf_tensor("wq_sb", [128, HPC * KC * 128], BF16))
                wk_sb = QE(nc.sbuf_tensor("wk_sb", [128, HPC * KC * 128], BF16))
                wv_sb = QE(nc.sbuf_tensor("wv_sb", [128, HPC * KC * 128], BF16))
                xnb = QE(nc.sbuf_tensor("xnb", [128, 2 * KC * 512], BF16))
                vstage = QE(nc.sbuf_tensor("vstage", [128, HPC * 512], BF16))
                p_q = QE(nc.psum_tensor("p_q", [128, HPC * 512], F32))
                p_k = QE(nc.psum_tensor("p_k", [128, HPC * 512], F32))
                p_v = QE(nc.psum_tensor("p_v", [128, HPC * 512], F32))
                p_vt = QE(nc.psum_tensor("p_vt", [128, 512], BF16))
                p.barrier(["gpsimd", "sync"])
                for dst, src in ((wq_sb, wq), (wk_sb, wk), (wv_sb, wv)):
                    p.op("gpsimd", lambda e, d=dst, s=src: e.dma_start(
                        out=d[:, :], in_=s), sem="WQKV", n=16)
                GLD_W = p.val("WQKV")

                psum_of = {"q": p_q, "k": p_k, "v": p_v}
                w_of = {"q": wq_sb, "k": wk_sb, "v": wv_sb}
                proj_rel = {(pr, s): 0 for pr in "qkv" for s in range(HPC)}
                vst_rel = [0, 0]
                xnb_rel = [0, 0]
                for tb in range(NQC):
                    r, co = tb // 2, (tb % 2) * 512
                    xb = tb % 2
                    ld = p.op("sync", lambda e, r=r, co=co, xb=xb: e.dma_start(
                        out=xnb.ap().rearrange("p (b k t) -> p b k t", b=2, k=KC)
                        [:, xb, :, :],
                        in_=xn_all.rearrange("(r k p) t -> r p k t", k=KC, p=128)
                        [r, :, :, co:co + 512]), sem=f"XNB{xb}", n=16,
                        waits=[("CCA", CC_AG), ("PE", xnb_rel[xb])])
                    for pr in ("q", "k", "v"):
                        for s in range(HPC):
                            for k in range(KC):
                                wts = [(f"XNB{xb}", ld), ("WQKV", GLD_W),
                                       ("SE", proj_rel[(pr, s)])] if k == 0 else ()
                                p.op("tensor", lambda e, pr=pr, s=s, k=k, xb=xb: e.matmul(
                                    psum_of[pr][:, s * 512:(s + 1) * 512],
                                    w_of[pr][:, (s * KC + k) * 128:(s * KC + k) * 128 + 128],
                                    xnb[:, (xb * KC + k) * 512:(xb * KC + k) * 512 + 512],
                                    start=(k == 0), stop=(k == KC - 1)),
                                    sem="PE" if k == KC - 1 else None, waits=wts)
                            pe_done = p.val("PE")
                            if pr == "q":
                                proj_rel[(pr, s)] = p.op("scalar",
                                    lambda e, s=s, tb=tb: e.activation(
                                        q_sb[:, s * SEQ + tb * 512:s * SEQ + tb * 512 + 512],
                                        p_q[:, s * 512:(s + 1) * 512], AF.Identity,
                                        bias=c_bqs[:, s:s + 1], scale=np.float32(ISCALE)),
                                    sem="SE", waits=[("PE", pe_done), ("VE", p.val("VE"))])
                            elif pr == "k":
                                proj_rel[(pr, s)] = p.op("scalar",
                                    lambda e, s=s, tb=tb: e.activation(
                                        k_sb[:, s * SEQ + tb * 512:s * SEQ + tb * 512 + 512],
                                        p_k[:, s * 512:(s + 1) * 512], AF.Identity,
                                        bias=c_bk[:, s:s + 1]),
                                    sem="SE", waits=[("PE", pe_done)])
                            else:
                                vwts = [("PE", pe_done), ("VE", vst_rel[s])]
                                if tb == 0 and s == 0:
                                    vwts += [("SP0", p.val("SP0")),
                                             ("SP1", p.val("SP1"))]
                                proj_rel[(pr, s)] = p.op("scalar",
                                    lambda e, s=s: e.activation(
                                        vstage[:, s * 512:(s + 1) * 512],
                                        p_v[:, s * 512:(s + 1) * 512], AF.Identity,
                                        bias=c_bv[:, s:s + 1]),
                                    sem="SE", waits=vwts)
                    xnb_rel[xb] = p.val("PE")
                    for s in range(HPC):
                        se_v = proj_rel[("v", s)]
                        for i in range(4):
                            p.op("tensor", lambda e, s=s, i=i: e.transpose(
                                p_vt[:, i * 128:(i + 1) * 128],
                                vstage[:, s * 512 + i * 128:s * 512 + i * 128 + 128],
                                c_id[:, :]), sem="PE" if i == 3 else None,
                                waits=([("SE", se_v), ("VE", p.val("VE"))] if i == 0 else ()))
                        tpe = p.val("PE")
                        vst_rel[s] = p.op("vector", lambda e, s=s, tb=tb: e.tensor_copy(
                            vt_sb[:, (s * 64 + tb * 4) * 128:(s * 64 + tb * 4) * 128 + 512],
                            p_vt[:, :]), sem="VE", waits=[("PE", tpe)])
                if "t_q" in taps:
                    p.op("sync", lambda e: e.dma_start(out=tap_d["t_q"], in_=q_sb[:, :]),
                         sem="TAP", n=16, waits=[("SE", p.val("SE"))])
                if "t_k" in taps:
                    p.op("sync", lambda e: e.dma_start(out=tap_d["t_k"], in_=k_sb[:, :]),
                         sem="TAP", n=16, waits=[("SE", p.val("SE"))])
                if "t_vt" in taps:
                    p.op("sync", lambda e: e.dma_start(out=tap_d["t_vt"], in_=vt_sb[:, :]),
                         sem="TAP", n=16, waits=[("VE", p.val("VE"))])
                p.flush(block)

            z_stack = ExitStack()
            if level >= 2:
                z_sb = z_stack.enter_context(
                    nc.sbuf_tensor("z_sb", [128, HPC * SEQ], BF16))

            if level >= 2:
              with ExitStack() as attn_ctx:
                AE = attn_ctx.enter_context
                pt_sb = AE(nc.sbuf_tensor("pt_sb", [128, 4 * 512], BF16))
                c_mask_s = AE(nc.sbuf_tensor("c_mask_s", [128, 4 * 512], BF16))
                mstats = AE(nc.sbuf_tensor("mstats", [128, 64], BF16))
                mcol = AE(nc.sbuf_tensor("mcol", [128, 1], F32))
                negm = AE(nc.sbuf_tensor("negm", [1, 2], F32))
                negmb = AE(nc.sbuf_tensor("negmb", [1, 2], BF16))
                biasM = AE(nc.sbuf_tensor("biasM", [128, 2], F32))
                sumr = AE(nc.sbuf_tensor("sumr", [1, 512], F32))
                rsum = AE(nc.sbuf_tensor("rsum", [1, 512], F32))
                bcsum = AE(nc.sbuf_tensor("bcsum", [128, 512], F32))
                sums_hist = AE(nc.sbuf_tensor("sums_hist", [1, 4 * 512], F32)) \
                    if "t_sums" in taps else None
                sums_slots = {14: 0, 15: 1, 30: 2, 31: 3}
                p_sa = AE(nc.psum_tensor("p_sa", [128, 1024], F32))
                p_st = AE(nc.psum_tensor("p_st", [128, 3 * 512], F32))
                p_z = AE(nc.psum_tensor("p_z", [128, 2 * 512], F32))
                p_row = AE(nc.psum_tensor("p_row", [33, 512], F32))

                p.barrier(["sync", "tensor", "scalar", "vector"])
                p.op("sync", lambda e: e.dma_start(out=c_mask_s[:, :], in_=c_mask_d),
                     sem="MASK", n=16)
                LD_MASK = p.val("MASK")

                state = {
                    "sa_rel": [0, 0],
                    "st_rel": [0, 0, 0],
                    "pt_rel": [0, 0, 0, 0],
                    "pz_rel": [("VE", 0), ("VE", 0)],
                    "prow_rel": [0, 0],
                    "pbc_rel": 0,
                    "mstats_rel": 0,
                    "biasM_ready": [0, 0],
                    "mcol_rel": 0,
                    "negmb_rel": [0, 0],
                    "mask_ok": LD_MASK,
                }

                def passA_units(i):
                    s, qc = divmod(i, NQC)
                    units = []
                    nblk = 0
                    blocks = []
                    for qt in range(4):
                        for kb in range(qc + 1):
                            blocks.append((qt, kb))
                    bi = 0
                    while bi < len(blocks):
                        pair = blocks[bi:bi + 2]
                        bi += len(pair)

                        def unit(pair=pair, i=i, s=s, qc=qc, base=nblk):
                            pe_vals = []
                            for j, (qt, kb) in enumerate(pair):
                                v = p.op("tensor", lambda e, s=s, qc=qc, qt=qt, kb=kb, j=j:
                                    e.matmul(p_sa[:, j * 512:(j + 1) * 512],
                                             q_sb[:, s * SEQ + qc * 512 + qt * 128:
                                                  s * SEQ + qc * 512 + qt * 128 + 128],
                                             k_sb[:, s * SEQ + kb * 512:
                                                  s * SEQ + kb * 512 + 512],
                                             start=True, stop=True),
                                    sem="PE",
                                    waits=[("VE", state["sa_rel"][j])])
                                pe_vals.append(v)
                            w = max(pe_vals)
                            width = 512 * len(pair)
                            r = p.op("vector", lambda e, base=base, width=width:
                                e.tensor_reduce(mstats[:, base:base + 1],
                                                p_sa[:, 0:width], AX, ALU.max),
                                sem="VE", waits=[("PE", w), ("VE", state["mstats_rel"])])
                            state["sa_rel"][0] = r
                            if len(pair) > 1:
                                state["sa_rel"][1] = r

                        units.append(unit)
                        nblk += 1

                    def finalize(i=i, nblk=nblk):
                        sl = i % 2
                        v1 = p.op("vector", lambda e, nblk=nblk: e.tensor_reduce(
                            mcol[:, :], mstats[:, 0:nblk], AX, ALU.max), sem="VE",
                            waits=[("VE", p.val("VE")), ("PE", state["mcol_rel"])])
                        state["mstats_rel"] = v1
                        t1 = p.op("tensor", lambda e: e.transpose(
                            p_sa[0:1, 0:128], mcol[:, :], c_idf[:, :]), sem="PE",
                            waits=([("C", C_ALL)] if i == 0 else []) +
                            [("VE", v1), ("VE", state["sa_rel"][0])])
                        state["mcol_rel"] = t1
                        v2 = p.op("vector", lambda e, sl=sl: e.tensor_reduce(
                            negm[:, sl:sl + 1], p_sa[0:1, 0:128], AX, ALU.max,
                            negate=True),
                            sem="VE", waits=[("PE", t1)])
                        state["sa_rel"][0] = v2
                        v3 = p.op("vector", lambda e, sl=sl: e.tensor_copy(
                            negmb[:, sl:sl + 1], negm[:, sl:sl + 1]), sem="VE",
                            waits=[("VE", v2), ("PE", state["negmb_rel"][sl])])
                        j = i % 2
                        t2 = p.op("tensor", lambda e, sl=sl, j=j: e.matmul(
                            p_z[:, j * 512:j * 512 + 1], c_on_rb[:, :],
                            negmb[:, sl:sl + 1],
                            start=True, stop=True), sem="PE",
                            waits=[("VE", v3), ("SE", state["pbc_rel"]),
                                   state["pz_rel"][j]])
                        state["negmb_rel"][sl] = t2
                        state["biasM_ready"][sl] = p.op("scalar",
                            lambda e, sl=sl, j=j: e.activation(
                                biasM[:, sl:sl + 1], p_z[:, j * 512:j * 512 + 1],
                                AF.Copy),
                            sem="SE", waits=[("PE", t2)])
                        state["pbc_rel"] = state["biasM_ready"][sl]
                        state["pz_rel"][j] = ("SE", state["biasM_ready"][sl])

                    return units, finalize

                def passB_units(i):
                    s, qc = divmod(i, NQC)
                    nkt = (qc + 1) * 4
                    qoff = s * SEQ + qc * 512
                    zsl = i % 2
                    units = []

                    def st_mm(kt):
                        sl = kt % 3
                        wts = [("SE", state["st_rel"][sl])]
                        return p.op("tensor", lambda e, kt=kt, sl=sl: e.matmul(
                            p_st[:, sl * 512:(sl + 1) * 512],
                            k_sb[:, s * SEQ + kt * 128:s * SEQ + kt * 128 + 128],
                            q_sb[:, qoff:qoff + 512], start=True, stop=True),
                            sem="PE", waits=wts)

                    pe_st = {}

                    def unit(kt, nkt=nkt, s=s, qc=qc, i=i):
                        if kt == 0:
                            pe_st[0] = st_mm(0)
                            if nkt > 1:
                                pe_st[1] = st_mm(1)
                        sl = kt % 3
                        ps = kt % 4
                        se = p.op("scalar", lambda e, kt=kt, sl=sl, ps=ps, i=i:
                            e.activation(pt_sb[:, ps * 512:(ps + 1) * 512],
                                         p_st[:, sl * 512:(sl + 1) * 512], AF.Exp,
                                         bias=biasM[:, i % 2:i % 2 + 1]),
                            sem="SE", waits=[("PE", pe_st[kt]),
                                             ("PE", state["pt_rel"][ps]),
                                             ("SE", state["biasM_ready"][i % 2])])
                        state["st_rel"][sl] = se
                        ready = ("SE", se)
                        i_d = kt - qc * 4
                        if i_d >= 0:
                            dv = p.op("vector", lambda e, ps=ps, i_d=i_d: e.tensor_tensor(
                                pt_sb[:, ps * 512:(ps + 1) * 512],
                                pt_sb[:, ps * 512:(ps + 1) * 512],
                                c_mask_s[:, i_d * 512:(i_d + 1) * 512], ALU.mult),
                                sem="VE", waits=[("SE", se), ("MASK", state["mask_ok"])])
                            ready = ("VE", dv)
                        if kt + 2 < nkt:
                            pe_st[kt + 2] = st_mm(kt + 2)
                        wts = [ready]
                        if kt == 0:
                            wts.append(("VE", state["prow_rel"][zsl]))
                        p.op("tensor", lambda e, kt=kt, ps=ps, nkt=nkt, zsl=zsl: e.matmul(
                            p_row[32 * zsl:32 * zsl + 1, :], c_on_c[:, :],
                            pt_sb[:, ps * 512:(ps + 1) * 512],
                            start=(kt == 0), stop=(kt == nkt - 1)),
                            sem=None, waits=wts)
                        wts2 = []
                        if kt == 0:
                            wts2.append(state["pz_rel"][zsl])
                        state["pt_rel"][ps] = p.op("tensor",
                            lambda e, kt=kt, ps=ps, s=s, nkt=nkt, zsl=zsl: e.matmul(
                                p_z[:, zsl * 512:(zsl + 1) * 512],
                                vt_sb[:, (s * 64 + kt) * 128:(s * 64 + kt) * 128 + 128],
                                pt_sb[:, ps * 512:(ps + 1) * 512],
                                start=(kt == 0), stop=(kt == nkt - 1)),
                            sem="PE", waits=wts2)

                    for kt in range(nkt):
                        units.append(lambda kt=kt: unit(kt))

                    def finalize(i=i, s=s, qc=qc, qoff=qoff, zsl=None):
                        zsl = i % 2
                        pe_last = p.val("PE")
                        v1 = p.op("vector", lambda e, zsl=zsl: e.tensor_copy(
                            sumr[:, :], p_row[32 * zsl:32 * zsl + 1, :]),
                            sem="VE", waits=[("PE", pe_last)])
                        state["prow_rel"][zsl] = v1
                        if sums_hist is not None and i in sums_slots:
                            sslot = sums_slots[i]
                            p.op("vector", lambda e, sslot=sslot: e.tensor_copy(
                                sums_hist[:, sslot * 512:(sslot + 1) * 512], sumr[:, :]),
                                sem="VE", waits=[("VE", v1)])
                        v2 = p.op("vector", lambda e: e.reciprocal(rsum[:, :], sumr[:, :]),
                                  sem="VE", waits=[("VE", v1)])
                        j = (i + 1) % 2
                        t1 = p.op("tensor", lambda e, j=j: e.matmul(
                            p_z[:, j * 512:(j + 1) * 512], c_on_rf[:, :], rsum[:, :],
                            start=True, stop=True),
                            sem="PE", waits=[("VE", v2), ("SE", state["pbc_rel"]),
                                             state["pz_rel"][j]])
                        s1 = p.op("scalar", lambda e, j=j: e.activation(
                            bcsum[:, :], p_z[:, j * 512:(j + 1) * 512], AF.Copy),
                            sem="SE", waits=[("PE", t1)])
                        state["pbc_rel"] = s1
                        state["pz_rel"][j] = ("SE", s1)
                        v3 = p.op("vector", lambda e, qoff=qoff, zsl=zsl: e.tensor_tensor(
                            z_sb[:, qoff:qoff + 512], p_z[:, zsl * 512:(zsl + 1) * 512],
                            bcsum[:, :], ALU.mult),
                            sem="VE", waits=[("SE", s1)])
                        state["pz_rel"][zsl] = ("VE", v3)

                    return units, finalize

                npair = HPC * NQC
                prevB = None
                pendingA_fin = None
                ua, fa = passA_units(0)
                for u in ua:
                    u()
                fa()
                for i in range(npair):
                    ub, fb = passB_units(i)
                    if i + 1 < npair:
                        ua, fa = passA_units(i + 1)
                    else:
                        ua, fa = [], None
                    if interleave:
                        nb, na = len(ub), len(ua)
                        ai = 0
                        for j, u in enumerate(ub):
                            u()
                            while ai < na and ai * nb < (j + 1) * na:
                                ua[ai]()
                                ai += 1
                            if ai == na and fa is not None:
                                fa()
                                fa = None
                        while ai < na:
                            ua[ai]()
                            ai += 1
                        if fa is not None:
                            fa()
                        fb()
                    else:
                        for u in ub:
                            u()
                        fb()
                        for u in ua:
                            u()
                        if fa is not None:
                            fa()

                if "t_z" in taps:
                    p.op("sync", lambda e: e.dma_start(out=tap_d["t_z"], in_=z_sb[:, :]),
                         sem="TAP", n=16, waits=[("VE", p.val("VE"))])
                if sums_hist is not None:
                    p.op("sync", lambda e: e.dma_start(out=tap_d["t_sums"], in_=sums_hist[:, :]),
                         sem="TAP", n=16, waits=[("VE", p.val("VE"))])
                p.flush(block)

            bar_attn_end = [(s, p.val(s)) for s in ("PE", "SE", "VE")]

            if level >= 3:
              with ExitStack() as wo_ctx:
                WE = wo_ctx.enter_context
                wo_sb = WE(nc.sbuf_tensor("wo_sb", [128, HPC * D], BF16))
                rs_stage = WE(nc.sbuf_tensor("rs_stage", [128, 2 * D], BF16))
                p_at = WE(nc.psum_tensor("p_at", [128, 2 * D], F32))
                p.barrier(["gpsimd", "tensor", "scalar", "sync"])
                p.op("gpsimd", lambda e: e.dma_start(out=wo_sb[:, :], in_=wo),
                     sem="WOL", n=16)
                GLD_WO = p.val("WOL")
                at_rel = [0, 0]
                stg_rel = [0, 0]
                for gt in range(SEQ // 128):
                    g = gt % 2
                    for nn in range(4):
                        for s in range(HPC):
                            wts = []
                            if nn == 0 and s == 0:
                                wts = [("SE", at_rel[g]), ("WOL", GLD_WO)]
                            p.op("tensor", lambda e, gt=gt, g=g, nn=nn, s=s: e.matmul(
                                p_at[:, g * D + nn * 512:g * D + nn * 512 + 512],
                                z_sb[:, s * SEQ + gt * 128:s * SEQ + gt * 128 + 128],
                                wo_sb[:, s * D + nn * 512:s * D + nn * 512 + 512],
                                start=(s == 0), stop=(s == HPC - 1)),
                                sem="PE" if (nn == 3 and s == HPC - 1) else None,
                                waits=wts)
                    pe_gt = p.val("PE")
                    at_rel[g] = p.op("scalar", lambda e, g=g: e.activation(
                        rs_stage[:, g * D:(g + 1) * D], p_at[:, g * D:(g + 1) * D],
                        AF.Copy), sem="SE",
                        waits=[("PE", pe_gt), (f"SR{g}", stg_rel[g])])
                    stg_rel[g] = p.op("sync", lambda e, gt=gt, g=g: e.dma_start(
                        out=rs_ctb[gt * 128:(gt + 1) * 128, :],
                        in_=rs_stage[:, g * D:(g + 1) * D]), sem=f"SR{g}", n=16,
                        waits=[("SE", at_rel[g])])
                SR0_ALL, SR1_ALL = p.val("SR0"), p.val("SR1")
                p.op("gpsimd", lambda e: e.collective_compute(
                    "ReduceScatter", ALU.add, replica_groups=[list(range(NCORE))],
                    ins=[rs_ctb.opt()], outs=[rs_sc.opt()]), sem="CCR", n=1,
                    waits=[("SR0", SR0_ALL), ("SR1", SR1_ALL)])
                CC_RS = p.val("CCR")
                if "t_rs" in taps:
                    p.op("sync", lambda e: e.dma_start(out=tap_d["t_rs"], in_=rs_sc),
                         sem="TAP", n=16, waits=[("CCR", CC_RS)])
                p.flush(block)
            z_stack.close()

        if level >= 4:
            emit_ffn([1], add_rs=True, bar=None)

        if level >= 4:
            with ExitStack() as fin_ctx:
                GE = fin_ctx.enter_context
                fpart = GE(nc.sbuf_tensor("fpart", [128, 2 * D], F32))
                frs = GE(nc.sbuf_tensor("frs", [128, 2 * D], BF16))
                fout = GE(nc.sbuf_tensor("fout", [128, 2 * D], F32))
                p.barrier(["sync", "vector"])
                fp_rel = [0, 0]
                fo_rel = [0, 0]
                for tt in range(NTT // 2):
                    sl = tt % 2
                    lp = p.op("sync", lambda e, tt=tt, sl=sl: e.dma_start(
                        out=fpart[:, sl * D:(sl + 1) * D],
                        in_=part_d[tt * 128:(tt + 1) * 128, :]), sem=f"FP{sl}", n=16,
                        waits=[("VE", fp_rel[sl])] + ([("SP0", p.val("SP0")),
                                                       ("SP1", p.val("SP1"))] if tt < 2 else []))
                    lr = p.op("sync", lambda e, tt=tt, sl=sl: e.dma_start(
                        out=frs[:, sl * D:(sl + 1) * D],
                        in_=rs_sc[tt * 128:(tt + 1) * 128, :]), sem=f"FR{sl}", n=16,
                        waits=[("CCR", CC_RS), ("VE", fp_rel[sl])])
                    v = p.op("vector", lambda e, sl=sl: e.tensor_tensor(
                        fout[:, sl * D:(sl + 1) * D], fpart[:, sl * D:(sl + 1) * D],
                        frs[:, sl * D:(sl + 1) * D], ALU.add), sem="VE",
                        waits=[(f"FP{sl}", lp), (f"FR{sl}", lr),
                               (f"SO{sl}", fo_rel[sl])])
                    fp_rel[sl] = v
                    fo_rel[sl] = p.op("sync", lambda e, tt=tt, sl=sl: e.dma_start(
                        out=out_d[tt * 128:(tt + 1) * 128, :],
                        in_=fout[:, sl * D:(sl + 1) * D]), sem=f"SO{sl}", n=16,
                        waits=[("VE", v)])
                if "t_rs" in taps:
                    p.op("sync", lambda e: e.dma_start(out=tap_d["t_rs"], in_=rs_sc),
                         sem="TAP", n=16, waits=[("CCR", CC_RS)])
                p.flush(block)
            z_stack.close()

        for s in ("STA", "SR0", "SR1", "SP0", "SP1", "SO0", "SO1", "TAP"):
            if s in p.cnt:
                p.wait("sync", s, p.val(s))
        p.flush(block)

    return nc



def _prep_inputs(L0_attn_x, L0_attn_W_Q, L0_attn_W_K, L0_attn_W_V, L0_attn_W_O,
                 L0_attn_b_Q, L0_attn_b_K, L0_attn_b_V, L0_attn_b_O,
                 L0_ffn_w_gate, L0_ffn_w_up, L0_ffn_w_down,
                 L0_ffn_s_gate, L0_ffn_s_up, L0_ffn_s_down):
    x = np.asarray(L0_attn_x, np.float32).reshape(SEQ, D)
    WQ = np.asarray(L0_attn_W_Q, np.float32)
    WK = np.asarray(L0_attn_W_K, np.float32)
    WV = np.asarray(L0_attn_W_V, np.float32)
    WO = np.asarray(L0_attn_W_O, np.float32)
    bQ = np.asarray(L0_attn_b_Q, np.float32)
    bK = np.asarray(L0_attn_b_K, np.float32)
    bV = np.asarray(L0_attn_b_V, np.float32)
    bO = np.asarray(L0_attn_b_O, np.float32).reshape(1, D)
    wg = np.asarray(L0_ffn_w_gate, np.int8)
    wu = np.asarray(L0_ffn_w_up, np.int8)
    wd = np.asarray(L0_ffn_w_down, np.int8)
    sgv = np.asarray(L0_ffn_s_gate, np.float32)
    suv = np.asarray(L0_ffn_s_up, np.float32)
    sd = np.asarray(L0_ffn_s_down, np.float32).reshape(1, D)

    consts = dict(
        c_id=np.eye(128, dtype=ml_dtypes.bfloat16),
        c_idf=np.eye(128, dtype=np.float32),
        c_on_c=np.ones((128, 1), ml_dtypes.bfloat16),
        c_on_rb=np.ones((1, 128), ml_dtypes.bfloat16),
        c_on_rf=np.ones((1, 128), np.float32),
        sg_i=np.ascontiguousarray(sgv.reshape(64, 128).T),
        su_i=np.ascontiguousarray(suv.reshape(64, 128).T),
        bo_r=bO, sd_r=sd, wg_d=wg, wu_d=wu, wd_d=wd,
    )
    mask = np.zeros((128, 4 * 512), ml_dtypes.bfloat16)
    row = np.arange(128)[:, None]
    col = np.arange(512)[None, :]
    for i in range(4):
        mask[:, i * 512:(i + 1) * 512] = (col >= i * 128 + row).astype(
            ml_dtypes.bfloat16)
    consts["c_mask"] = mask

    in_maps = []
    for c in range(NCORE):
        hs = slice(HPC * c, HPC * (c + 1))
        m = dict(consts)
        m["x_sh"] = np.ascontiguousarray(x[c * TOK:(c + 1) * TOK])
        m["wq"] = np.ascontiguousarray(
            WQ[hs].reshape(HPC, KC, 128, 128).transpose(2, 0, 1, 3).reshape(128, -1))
        m["wk"] = np.ascontiguousarray(
            WK[hs].reshape(HPC, KC, 128, 128).transpose(2, 0, 1, 3).reshape(128, -1))
        m["wv"] = np.ascontiguousarray(
            WV[hs].reshape(HPC, KC, 128, 128).transpose(2, 0, 1, 3).reshape(128, -1))
        m["wo"] = np.ascontiguousarray(
            WO[hs].transpose(1, 0, 2).reshape(128, -1))
        m["bq"] = np.ascontiguousarray(bQ[hs].T)
        m["bk"] = np.ascontiguousarray(bK[hs].T)
        m["bv"] = np.ascontiguousarray(bV[hs].T)
        in_maps.append(m)
    return in_maps


_NC_CACHE = {}


def kernel(**inputs):
    in_maps = _prep_inputs(**inputs)
    key = "main"
    if key not in _NC_CACHE:
        _NC_CACHE[key] = build()
    nc = _NC_CACHE[key]
    res = run_bass_kernel_spmd(nc, in_maps, core_ids=list(range(NCORE)))
    out = np.concatenate([np.asarray(r["out"]) for r in res.results], axis=0)
    return out.reshape(1, SEQ, D)


# version 20 (final)
# speedup vs baseline: 1.0156x; 1.0043x over previous
import math
import numpy as np
import ml_dtypes
from contextlib import ExitStack

import concourse.bass as bass
import concourse.mybir as mybir
from concourse.bass_utils import run_bass_kernel_spmd

F32 = mybir.dt.float32
BF16 = mybir.dt.bfloat16
I8 = mybir.dt.int8
AF = mybir.ActivationFunctionType
ALU = mybir.AluOpType
AX = mybir.AxisListType.X

SEQ = 8192
D = 2048
NH = 16
DH = 128
INTER = 8192
NCORE = 8
TOK = SEQ // NCORE
HPC = NH // NCORE
KC = D // 128
NQC = SEQ // 512
ISCALE = 1.0 / math.sqrt(DH)
TC = 512
NTT = TOK // 128


class Plan:

    ENGS = ("sync", "tensor", "vector", "scalar", "gpsimd")

    def __init__(self, nc, ctx):
        self.q = {e: [] for e in self.ENGS}
        self.cnt = {}
        self.handles = {}
        self._nc = nc
        self._ctx = ctx

    def sem(self, name):
        if name not in self.handles:
            self.handles[name] = self._ctx.enter_context(
                self._nc.semaphore("sem" + name))
            self.cnt[name] = 0
        return name

    def inc(self, sem, n=1):
        self.sem(sem)
        self.cnt[sem] += n
        return self.cnt[sem]

    def val(self, sem):
        self.sem(sem)
        return self.cnt[sem]

    def emit(self, eng, fn):
        self.q[eng].append(fn)

    def op(self, eng, make, sem=None, n=1, waits=()):
        target = self.inc(sem, n) if sem else None
        waits = tuple(waits)
        for ws, wv in waits:
            self.sem(ws)

        def fn(e, make=make, sem=sem, n=n, waits=waits):
            for ws, wv in waits:
                if wv > 0:
                    e.wait_ge(self.handles[ws], wv)
            ins = make(e)
            if sem is not None:
                ins.then_inc(self.handles[sem], n)

        self.emit(eng, fn)
        return target

    def wait(self, eng, sem, value):
        self.sem(sem)
        if value <= 0:
            return

        def fn(e, sem=sem, value=value):
            e.wait_ge(self.handles[sem], value)

        self.emit(eng, fn)

    def barrier(self, engs):
        vals = [(s, self.val(s)) for s in ("PE", "SE", "VE")]
        for eng in engs:
            for s, v in vals:
                self.wait(eng, s, v)

    def flush(self, block):
        for eng_name in self.ENGS:
            q = self.q[eng_name]
            if not q:
                continue
            getattr(block, eng_name)(lambda e, q=tuple(q): [f(e) for f in q])
            self.q[eng_name] = []


def build(taps=(), level=4, interleave=True):
    taps = set(taps)
    nc = bass.Bass("TRN2", target_bir_lowering=False, debug=False)

    x_sh = nc.dram_tensor("x_sh", [TOK, D], F32, kind="ExternalInput").ap()
    wq = nc.dram_tensor("wq", [128, HPC * KC * 128], F32, kind="ExternalInput").ap()
    wk = nc.dram_tensor("wk", [128, HPC * KC * 128], F32, kind="ExternalInput").ap()
    wv = nc.dram_tensor("wv", [128, HPC * KC * 128], F32, kind="ExternalInput").ap()
    wo = nc.dram_tensor("wo", [128, HPC * D], F32, kind="ExternalInput").ap()
    bq = nc.dram_tensor("bq", [128, HPC], F32, kind="ExternalInput").ap()
    bk = nc.dram_tensor("bk", [128, HPC], F32, kind="ExternalInput").ap()
    bv = nc.dram_tensor("bv", [128, HPC], F32, kind="ExternalInput").ap()
    bo_r = nc.dram_tensor("bo_r", [1, D], F32, kind="ExternalInput").ap()
    sg_i = nc.dram_tensor("sg_i", [128, 64], F32, kind="ExternalInput").ap()
    su_i = nc.dram_tensor("su_i", [128, 64], F32, kind="ExternalInput").ap()
    sd_r = nc.dram_tensor("sd_r", [1, D], F32, kind="ExternalInput").ap()
    wg_d = nc.dram_tensor("wg_d", [D, INTER], I8, kind="ExternalInput").ap()
    wu_d = nc.dram_tensor("wu_d", [D, INTER], I8, kind="ExternalInput").ap()
    wd_d = nc.dram_tensor("wd_d", [INTER, D], I8, kind="ExternalInput").ap()
    c_id_d = nc.dram_tensor("c_id", [128, 128], BF16, kind="ExternalInput").ap()
    c_idf_d = nc.dram_tensor("c_idf", [128, 128], F32, kind="ExternalInput").ap()
    c_on_c_d = nc.dram_tensor("c_on_c", [128, 1], BF16, kind="ExternalInput").ap()
    c_on_rb_d = nc.dram_tensor("c_on_rb", [1, 128], BF16, kind="ExternalInput").ap()
    c_on_rf_d = nc.dram_tensor("c_on_rf", [1, 128], F32, kind="ExternalInput").ap()
    c_mask_d = nc.dram_tensor("c_mask", [128, 4 * 512], BF16, kind="ExternalInput").ap()

    out_d = nc.dram_tensor("out", [TOK, D], F32, kind="ExternalOutput").ap()
    tap_d = {}
    for t in taps:
        shape = {"t_xn": [128, KC * TOK], "t_q": [128, HPC * SEQ],
                 "t_k": [128, HPC * SEQ], "t_vt": [128, HPC * SEQ],
                 "t_z": [128, HPC * SEQ], "t_rs": [TOK, D],
                 "t_sums": [1, 4 * 512], "t_dbg": [128, 2 * D],
                 "t_fuse": [128, (INTER // 128) * TC]}[t]
        dt = F32 if t == "t_sums" else BF16
        tap_d[t] = nc.dram_tensor(t, shape, dt, kind="ExternalOutput").ap()

    xn_ctb = nc.dram_tensor("xn_ctb", [D, TOK], BF16).ap()
    xn_all = nc.dram_tensor("xn_all", [NCORE * D, TOK], BF16).ap()
    rs_ctb = nc.dram_tensor("rs_ctb", [SEQ, D], BF16).ap()
    rs_sc = nc.dram_tensor("rs_sc", [TOK, D], BF16).ap()
    part_d = nc.dram_tensor("part_d", [TOK, D], F32).ap()
    bc_d = nc.dram_tensor("bc_d", [128, 2 * D], F32).ap()

    with ExitStack() as top:
        E = top.enter_context
        p = Plan(nc, top)
        block = E(nc.Block())

        c_id = E(nc.sbuf_tensor("c_id_s", [128, 128], BF16))
        c_idf = E(nc.sbuf_tensor("c_idf_s", [128, 128], F32))
        c_on_c = E(nc.sbuf_tensor("c_on_c_s", [128, 1], BF16))
        c_on_rb = E(nc.sbuf_tensor("c_on_rb_s", [1, 128], BF16))
        c_on_rf = E(nc.sbuf_tensor("c_on_rf_s", [1, 128], F32))
        c_bq = E(nc.sbuf_tensor("c_bq", [128, HPC], F32))
        c_bk = E(nc.sbuf_tensor("c_bk", [128, HPC], F32))
        c_bv = E(nc.sbuf_tensor("c_bv", [128, HPC], F32))
        c_bqs = E(nc.sbuf_tensor("c_bqs", [128, HPC], F32))
        c_sg = E(nc.sbuf_tensor("c_sg", [128, 64], F32))
        c_su = E(nc.sbuf_tensor("c_su", [128, 64], F32))
        xn_mine = E(nc.sbuf_tensor("xn_mine", [128, KC * TOK], BF16))

        for dst, src in ((c_id, c_id_d), (c_idf, c_idf_d), (c_on_c, c_on_c_d), (c_on_rb, c_on_rb_d),
                         (c_on_rf, c_on_rf_d), (c_bq, bq), (c_bk, bk), (c_bv, bv),
                         (c_sg, sg_i), (c_su, su_i)):
            p.op("sync", lambda e, d=dst, s=src: e.dma_start(out=d[:, :], in_=s),
                 sem="C", n=16)
        LD_CONST = p.val("C")

        with ExitStack() as ph0_ctx:
            rows = ph0_ctx.enter_context(nc.sbuf_tensor("ph0_rows", [1, 2 * D], F32))
            bc_tmp = ph0_ctx.enter_context(nc.sbuf_tensor("bc_tmp", [128, 2 * D], F32))
            p_tmp = ph0_ctx.enter_context(nc.psum_tensor("p_tmp", [128, D], F32))
            c_bo_bc = bc_tmp[:, 0:D]
            c_sd_bc = bc_tmp[:, D:2 * D]
            p.op("sync", lambda e: e.dma_start(out=rows[:, 0:D], in_=bo_r), sem="C", n=16)
            p.op("sync", lambda e: e.dma_start(out=rows[:, D:2 * D], in_=sd_r), sem="C", n=16)
            C_ALL = p.val("C")
            p.op("vector", lambda e: e.tensor_scalar_mul(c_bqs[:, :], c_bq[:, :], ISCALE),
                 sem="VE", waits=[("C", C_ALL)])
            for j in range(4):
                p.op("tensor", lambda e, j=j: e.matmul(
                    p_tmp[:, j * 512:(j + 1) * 512], c_on_rf[:, :],
                    rows[:, j * 512:(j + 1) * 512], start=True, stop=True),
                    sem="PE", waits=[("C", C_ALL)] if j == 0 else ())
            PE_BO = p.val("PE")
            p.op("scalar", lambda e: e.activation(c_bo_bc, p_tmp[:, :], AF.Copy),
                 sem="SE", waits=[("PE", PE_BO)])
            SE_BO = p.val("SE")
            for j in range(4):
                p.op("tensor", lambda e, j=j: e.matmul(
                    p_tmp[:, j * 512:(j + 1) * 512], c_on_rf[:, :],
                    rows[:, D + j * 512:D + (j + 1) * 512], start=True, stop=True),
                    sem="PE", waits=[("SE", SE_BO)] if j == 0 else ())
            PE_SD = p.val("PE")
            p.op("scalar", lambda e: e.activation(c_sd_bc, p_tmp[:, :], AF.Copy),
                 sem="SE", waits=[("PE", PE_SD)])
            SE_SD = p.val("SE")
            st_bc = p.op("sync", lambda e: e.dma_start(out=bc_d, in_=bc_tmp[:, :]),
                         sem="BCD", n=16, waits=[("SE", SE_SD)])

            with ExitStack() as ln_ctx:
                LE = ln_ctx.enter_context
                xs = LE(nc.sbuf_tensor("xs", [128, 2 * D], F32))
                xn_tm = LE(nc.sbuf_tensor("xn_tm", [128, D], BF16))
                sqj = LE(nc.sbuf_tensor("sqj", [128, D], BF16))
                ln_sc = LE(nc.sbuf_tensor("ln_sc", [128, 8], F32))
                p_xt = LE(nc.psum_tensor("p_xt", [128, D], BF16))
                xs_rel = [0, 0]
                tm_rel = 0
                pxt_rel = 0
                for xt in range(NTT):
                    sl = xt % 2
                    ld = p.op("sync", lambda e, xt=xt, sl=sl: e.dma_start(
                        out=xs[:, sl * D:(sl + 1) * D],
                        in_=x_sh[xt * 128:(xt + 1) * 128, :]), sem=f"X{sl}", n=16,
                        waits=[("SE", xs_rel[sl])])
                    xv = lambda: xs[:, sl * D:(sl + 1) * D]
                    v1 = p.op("vector", lambda e, sl=sl: e.tensor_reduce(
                        ln_sc[:, 0:1], xs[:, sl * D:(sl + 1) * D], AX, ALU.add),
                        sem="VE", waits=[(f"X{sl}", ld)])
                    s1 = p.op("scalar", lambda e, sl=sl: e.activation(
                        sqj[:, :], xs[:, sl * D:(sl + 1) * D], AF.Square,
                        accum_out=ln_sc[:, 2:3]), sem="SE",
                        waits=[(f"X{sl}", ld), ("VE", pxt_rel)])
                    v2 = p.op("vector", lambda e: e.tensor_scalar_mul(
                        ln_sc[:, 1:2], ln_sc[:, 0:1], 1.0 / D), sem="VE",
                        waits=[("VE", v1)])
                    v3 = p.op("vector", lambda e: e.tensor_scalar_mul(
                        ln_sc[:, 3:4], ln_sc[:, 2:3], 1.0 / D), sem="VE",
                        waits=[("SE", s1), ("VE", v2)])
                    v4 = p.op("vector", lambda e: e.tensor_tensor(
                        ln_sc[:, 4:5], ln_sc[:, 1:2], ln_sc[:, 1:2], ALU.mult),
                        sem="VE", waits=[("VE", v3)])
                    v5 = p.op("vector", lambda e: e.tensor_tensor(
                        ln_sc[:, 3:4], ln_sc[:, 3:4], ln_sc[:, 4:5], ALU.subtract),
                        sem="VE", waits=[("VE", v4)])
                    s2 = p.op("scalar", lambda e: e.activation(
                        ln_sc[:, 4:5], ln_sc[:, 3:4], AF.Ln, bias=np.float32(1e-5)), sem="SE",
                        waits=[("VE", v5)])
                    s3 = p.op("scalar", lambda e: e.activation(
                        ln_sc[:, 5:6], ln_sc[:, 4:5], AF.Exp, scale=np.float32(-0.5)), sem="SE",
                        waits=[("SE", s2)])
                    v6 = p.op("vector", lambda e: e.tensor_tensor(
                        ln_sc[:, 6:7], ln_sc[:, 1:2], ln_sc[:, 5:6], ALU.mult),
                        sem="VE", waits=[("SE", s3), ("VE", v5)])
                    v7 = p.op("vector", lambda e: e.tensor_scalar_mul(
                        ln_sc[:, 7:8], ln_sc[:, 6:7], -1.0), sem="VE",
                        waits=[("VE", v6)])
                    s4 = p.op("scalar", lambda e, sl=sl: e.activation(
                        xn_tm[:, :], xs[:, sl * D:(sl + 1) * D], AF.Identity,
                        bias=ln_sc[:, 7:8], scale=ln_sc[:, 5:6]), sem="SE",
                        waits=[("VE", v7), ("PE", tm_rel)])
                    xs_rel[sl] = s4
                    for k in range(KC):
                        p.op("tensor", lambda e, k=k: e.transpose(
                            p_xt[:, k * 128:(k + 1) * 128],
                            xn_tm[:, k * 128:(k + 1) * 128], c_id[:, :]),
                            sem="PE",
                            waits=([("SE", s4), ("VE", pxt_rel)] if k == 0 else ()))
                    tm_rel = p.val("PE")
                    pxt_rel = p.op("vector", lambda e, xt=xt: e.tensor_copy(
                        xn_mine.ap().rearrange("p (k t) -> p k t", k=KC)
                        [:, :, xt * 128:(xt + 1) * 128],
                        p_xt.ap().rearrange("p (k t) -> p k t", k=KC)),
                        sem="VE", waits=[("PE", tm_rel)])
                VE_XN = p.val("VE")
                st1 = p.op("sync", lambda e: e.dma_start(
                    out=xn_ctb.rearrange("(k p) t -> p k t", p=128),
                    in_=xn_mine.ap().rearrange("p (k t) -> p k t", k=KC)),
                    sem="STA", n=16, waits=[("VE", VE_XN)])
                if "t_xn" in taps:
                    p.op("sync", lambda e: e.dma_start(out=tap_d["t_xn"], in_=xn_mine[:, :]),
                         sem="TAP", n=16, waits=[("VE", VE_XN)])
                p.flush(block)

        def plan_ag():
            p.op("gpsimd", lambda e: e.collective_compute(
                "AllGather", ALU.bypass, replica_groups=[list(range(NCORE))],
                ins=[xn_ctb.opt()], outs=[xn_all.opt()]), sem="CCA", n=1,
                waits=[("STA", st1)])

        if level >= 1 and level < 4:
            plan_ag()

        def emit_ffn(tcs, add_rs=False, bar=None, ag_hook=None):
            sfx = str(tcs[0])
            hookbox = [ag_hook]
            with ExitStack() as ffn_ctx:
              FE = ffn_ctx.enter_context
              wstream = FE(nc.sbuf_tensor("wstream" + sfx, [128, 3 * 8192], BF16))
              xld = FE(nc.sbuf_tensor("xld" + sfx, [128, 2 * D], F32))
              bcs = FE(nc.sbuf_tensor("bcs" + sfx, [128, 2 * D], F32))
              rs_t = FE(nc.sbuf_tensor("rs_t" + sfx, [128, D], BF16)) if add_rs else None
              fuse = FE(nc.sbuf_tensor("fuse" + sfx, [128, (INTER // 128) * TC], BF16))
              u_sb = FE(nc.sbuf_tensor("u_sb" + sfx, [128, 8 * TC], BF16))
              out_sb = FE(nc.sbuf_tensor("out_sb" + sfx, [128, 2 * D], F32))

              if bar is None:
                  p.barrier(["gpsimd", "sync", "tensor", "scalar", "vector"])
              else:
                  for eng in ("gpsimd", "sync"):
                      for s, v in bar:
                          p.wait(eng, s, v)
                  p.barrier(["tensor", "scalar", "vector"])
              for eng in ("scalar", "vector"):
                  for s in ("SP0", "SP1", "SR0", "SR1", "TAP"):
                      p.wait(eng, s, p.val(s))
              ldbc = p.op("sync", lambda e: e.dma_start(out=bcs[:, :], in_=bc_d),
                          sem=f"BCL{sfx}", n=16, waits=[("BCD", p.val("BCD"))])
              wst_rel = [0, 0, 0]
              pf_rel = [0, 0]
              gu_rel = 0
              fuse_rel = 0
              xld_rel = [0, 0]
              rst_rel = [0]
              out_rel = [0, 0]
              wslot = 0
              for tc in tcs:
                  with nc.psum_tensor(f"p_g{tc}", [128, 8 * 512], F32) as p_g:
                      ev_rel = [0] * 8
                      for sl in range(8):
                          for wi, (wdr, sc_ap, dst) in enumerate(
                                  ((wg_d, c_sg, None), (wu_d, c_su, u_sb))):
                              lds = []
                              for kh in range(2):
                                  ws = wslot
                                  wslot = (wslot + 1) % 3
                                  lds.append((ws, p.op("gpsimd",
                                      lambda e, wdr=wdr, kh=kh, sl=sl, ws=ws: e.dma_start(
                                          out=wstream.ap().rearrange(
                                              "p (b k f) -> p b k f", b=3, k=8)
                                          [:, ws, :, :],
                                          in_=wdr.rearrange("(kc p) f -> p kc f", p=128)
                                          [:, kh * 8:(kh + 1) * 8, sl * 1024:(sl + 1) * 1024]),
                                      sem=f"W{ws}", n=16,
                                      waits=([("PE", wst_rel[ws])] if wst_rel[ws]
                                             else [("BCD", p.val("BCD"))]))))
                              if hookbox[0] is not None:
                                  hookbox[0]()
                                  hookbox[0] = None
                              for kh in range(2):
                                  ws, ld = lds[kh]
                                  for m in range(8):
                                      for k in range(8):
                                          wts = []
                                          if m == 0 and k == 0:
                                              wts.append((f"W{ws}", ld))
                                          if k == 0 and m == 0 and kh == 0:
                                              if wi == 0:
                                                  wts.append(("VE", gu_rel))
                                                  if sl == 0:
                                                      wts.append(("VE", pf_rel[0]))
                                                      wts.append(("VE", pf_rel[1]))
                                          if k == 0 and kh == 0 and wi == 1:
                                              wts.append(("SE", ev_rel[m]))
                                          last_half = (m == 7 and k == 7)
                                          last_bank = (kh == 1 and k == 7)
                                          p.op("tensor",
                                              lambda e, ws=ws, k=k, m=m, kh=kh, tc=tc:
                                              e.matmul(
                                                  p_g[:, m * 512:(m + 1) * 512],
                                                  wstream.ap().rearrange(
                                                      "p (b k f) -> p b k f", b=3, k=8)
                                                  [:, ws, k, m * 128:(m + 1) * 128],
                                                  xn_mine[:, (kh * 8 + k) * 1024 + tc * TC:
                                                          (kh * 8 + k) * 1024 + tc * TC + TC],
                                                  start=(kh == 0 and k == 0),
                                                  stop=(kh == 1 and k == 7)),
                                              sem="PE" if (last_half or last_bank) else None,
                                              waits=wts)
                                      if kh == 1:
                                          pe_m = p.val("PE")
                                          if wi == 0:
                                              ev_rel[m] = p.op("scalar",
                                                  lambda e, m=m, sl=sl: e.activation(
                                                      fuse[:, (sl * 8 + m) * TC:
                                                           (sl * 8 + m + 1) * TC],
                                                      p_g[:, m * 512:(m + 1) * 512], AF.Silu,
                                                      scale=c_sg[:, sl * 8 + m:sl * 8 + m + 1]),
                                                  sem="SE",
                                                  waits=[("PE", pe_m)] + ([("VE", gu_rel)] if m == 0 else []))
                                          else:
                                              ev_rel[m] = p.op("scalar",
                                                  lambda e, m=m, sl=sl: e.activation(
                                                      u_sb[:, m * TC:(m + 1) * TC],
                                                      p_g[:, m * 512:(m + 1) * 512], AF.Identity,
                                                      scale=c_su[:, sl * 8 + m:sl * 8 + m + 1]),
                                                  sem="SE", waits=[("PE", pe_m)])
                                  wst_rel[ws] = p.val("PE")
                              pe_gu = p.val("PE")
                          gu_rel = p.op("vector", lambda e, sl=sl: e.tensor_tensor(
                              fuse[:, sl * 8 * TC:(sl + 1) * 8 * TC],
                              fuse[:, sl * 8 * TC:(sl + 1) * 8 * TC],
                              u_sb[:, :], ALU.mult), sem="VE",
                              waits=[("SE", ev_rel[7]), ("PE", fuse_rel)])
                  VE_FUSE = p.val("VE")
                  if tc == 0 and "t_fuse" in taps:
                      p.op("sync", lambda e: e.dma_start(out=tap_d["t_fuse"], in_=fuse[:, :]),
                           sem="ST", n=16, waits=[("VE", VE_FUSE)])

                  with nc.psum_tensor(f"p_f{tc}", [128, 2 * D], F32) as p_f:
                      for pr in range(2):
                          tiles = (2 * pr, 2 * pr + 1)
                          ldx = {}
                          for tt in tiles:
                              slx = tt % 2
                              ldx[tt] = p.op("sync", lambda e, tc=tc, tt=tt, slx=slx:
                                  e.dma_start(out=xld[:, slx * D:(slx + 1) * D],
                                              in_=x_sh[tc * TC + tt * 128:
                                                       tc * TC + tt * 128 + 128, :]),
                                  sem=f"XL{slx}", n=16, waits=[("VE", xld_rel[slx])])
                          for mg in range(16):
                              ws = wslot
                              wslot = (wslot + 1) % 3
                              ld = p.op("gpsimd", lambda e, mg=mg, ws=ws: e.dma_start(
                                  out=wstream.ap().rearrange("p (b m f) -> p b m f",
                                                             b=3, m=4)[:, ws, :, :],
                                  in_=wd_d.rearrange("(mc p) f -> p mc f", p=128)
                                  [:, mg * 4:(mg + 1) * 4, :]), sem=f"W{ws}", n=16,
                                  waits=[("PE", wst_rel[ws])])
                              for mi in range(4):
                                  m = mg * 4 + mi
                                  for ti, tt in enumerate(tiles):
                                      for nn in range(4):
                                          wts = []
                                          if mi == 0 and ti == 0 and nn == 0:
                                              wts.append((f"W{ws}", ld))
                                              if mg == 0:
                                                  wts.append(("VE", VE_FUSE))
                                          if mi == 0 and nn == 0 and mg == 0:
                                              wts.append(("VE", pf_rel[ti]))
                                          p.op("tensor", lambda e, ws=ws, mi=mi, m=m, ti=ti,
                                              tt=tt, nn=nn, tc=tc: e.matmul(
                                                  p_f[:, ti * D + nn * 512:
                                                      ti * D + nn * 512 + 512],
                                                  fuse[:, m * TC + tt * 128:
                                                       m * TC + tt * 128 + 128],
                                                  wstream.ap().rearrange(
                                                      "p (b m f) -> p b m f", b=3, m=4)
                                                  [:, ws, mi, nn * 512:(nn + 1) * 512],
                                                  start=(m == 0), stop=(m == 63)),
                                              sem="PE" if (mi == 3 and ti == 1 and nn == 3)
                                              else None,
                                              waits=wts)
                              wst_rel[ws] = p.val("PE")
                          fuse_rel = p.val("PE")
                          pe_pair = p.val("PE")
                          for ti, tt in enumerate(tiles):
                              slx = tt % 2
                              slo = tt % 2
                              osem = f"SO{slo}" if add_rs else f"SP{slo}"
                              v1 = p.op("vector", lambda e, ti=ti, slo=slo: e.tensor_tensor(
                                  out_sb[:, slo * D:(slo + 1) * D],
                                  p_f[:, ti * D:(ti + 1) * D], bcs[:, D:2 * D], ALU.mult),
                                  sem="VE", waits=[("PE", pe_pair), (osem, out_rel[slo]),
                                                   (f"BCL{sfx}", ldbc)])
                              v2 = p.op("vector", lambda e, slo=slo: e.tensor_tensor(
                                  out_sb[:, slo * D:(slo + 1) * D],
                                  out_sb[:, slo * D:(slo + 1) * D], bcs[:, 0:D], ALU.add),
                                  sem="VE", waits=[("VE", v1)])
                              v3 = p.op("vector", lambda e, slo=slo, slx=slx: e.tensor_tensor(
                                  out_sb[:, slo * D:(slo + 1) * D],
                                  out_sb[:, slo * D:(slo + 1) * D],
                                  xld[:, slx * D:(slx + 1) * D], ALU.add),
                                  sem="VE", waits=[("VE", v2), (f"XL{slx}", ldx[tt])])
                              xld_rel[slx] = v3
                              if add_rs:
                                  ldr = p.op("sync", lambda e, tc=tc, tt=tt: e.dma_start(
                                      out=rs_t[:, :],
                                      in_=rs_sc[tc * TC + tt * 128:
                                                tc * TC + tt * 128 + 128, :]),
                                      sem="RST", n=16,
                                      waits=[("CCR", p.val("CCR")), ("VE", rst_rel[0])])
                                  v3 = p.op("vector", lambda e, slo=slo: e.tensor_tensor(
                                      out_sb[:, slo * D:(slo + 1) * D],
                                      out_sb[:, slo * D:(slo + 1) * D],
                                      rs_t[:, :], ALU.add),
                                      sem="VE", waits=[("VE", v3), ("RST", ldr)])
                                  rst_rel[0] = v3
                              pf_rel[ti] = v3
                              dsttn = out_d if add_rs else part_d
                              dsem = f"SO{slo}" if add_rs else f"SP{slo}"
                              out_rel[slo] = p.op("sync", lambda e, tc=tc, tt=tt, slo=slo,
                                  dsttn=dsttn: e.dma_start(
                                      out=dsttn[tc * TC + tt * 128:
                                                tc * TC + tt * 128 + 128, :],
                                      in_=out_sb[:, slo * D:(slo + 1) * D]),
                                  sem=dsem, n=16, waits=[("VE", v3)])
                          if add_rs and pr == 0:
                              for ft in range(4):
                                  slf = ft % 2
                                  lp = p.op("sync", lambda e, ft=ft, slf=slf: e.dma_start(
                                      out=out_sb[:, slf * D:(slf + 1) * D],
                                      in_=part_d[ft * 128:(ft + 1) * 128, :]),
                                      sem=f"FP{slf}", n=16,
                                      waits=[(f"SO{slf}", out_rel[slf]),
                                             ("SP0", p.val("SP0")),
                                             ("SP1", p.val("SP1"))])
                                  ldr = p.op("sync", lambda e, ft=ft: e.dma_start(
                                      out=rs_t[:, :],
                                      in_=rs_sc[ft * 128:(ft + 1) * 128, :]),
                                      sem="RST", n=16,
                                      waits=[("CCR", p.val("CCR")),
                                             ("VE", rst_rel[0])])
                                  v = p.op("vector", lambda e, slf=slf: e.tensor_tensor(
                                      out_sb[:, slf * D:(slf + 1) * D],
                                      out_sb[:, slf * D:(slf + 1) * D],
                                      rs_t[:, :], ALU.add),
                                      sem="VE", waits=[(f"FP{slf}", lp), ("RST", ldr)])
                                  rst_rel[0] = v
                                  out_rel[slf] = p.op("sync", lambda e, ft=ft, slf=slf:
                                      e.dma_start(
                                          out=out_d[ft * 128:(ft + 1) * 128, :],
                                          in_=out_sb[:, slf * D:(slf + 1) * D]),
                                      sem=f"SO{slf}", n=16, waits=[("VE", v)])



        if level >= 4:
            emit_ffn([0], ag_hook=plan_ag if level >= 1 else None)
        CC_AG = p.val("CCA")

        with ExitStack() as big_ctx:
            BE = big_ctx.enter_context
            q_sb = BE(nc.sbuf_tensor("q_sb", [128, HPC * SEQ], BF16))
            k_sb = BE(nc.sbuf_tensor("k_sb", [128, HPC * SEQ], BF16))
            vt_sb = BE(nc.sbuf_tensor("vt_sb", [128, HPC * SEQ], BF16))


            if level >= 1:
              with ExitStack() as qkv_ctx:
                QE = qkv_ctx.enter_context
                wq_sb = QE(nc.sbuf_tensor("wq_sb", [128, HPC * KC * 128], BF16))
                wk_sb = QE(nc.sbuf_tensor("wk_sb", [128, HPC * KC * 128], BF16))
                wv_sb = QE(nc.sbuf_tensor("wv_sb", [128, HPC * KC * 128], BF16))
                xnb = QE(nc.sbuf_tensor("xnb", [128, 2 * KC * 512], BF16))
                vstage = QE(nc.sbuf_tensor("vstage", [128, HPC * 512], BF16))
                p_q = QE(nc.psum_tensor("p_q", [128, HPC * 512], F32))
                p_k = QE(nc.psum_tensor("p_k", [128, HPC * 512], F32))
                p_v = QE(nc.psum_tensor("p_v", [128, HPC * 512], F32))
                p_vt = QE(nc.psum_tensor("p_vt", [128, 512], BF16))
                p.barrier(["gpsimd", "sync"])
                for dst, src in ((wq_sb, wq), (wk_sb, wk), (wv_sb, wv)):
                    p.op("gpsimd", lambda e, d=dst, s=src: e.dma_start(
                        out=d[:, :], in_=s), sem="WQKV", n=16)
                GLD_W = p.val("WQKV")

                psum_of = {"q": p_q, "k": p_k, "v": p_v}
                w_of = {"q": wq_sb, "k": wk_sb, "v": wv_sb}
                proj_rel = {(pr, s): 0 for pr in "qkv" for s in range(HPC)}
                vst_rel = [0, 0]
                xnb_rel = [0, 0]
                for tb in range(NQC):
                    r, co = tb // 2, (tb % 2) * 512
                    xb = tb % 2
                    ld = p.op("sync", lambda e, r=r, co=co, xb=xb: e.dma_start(
                        out=xnb.ap().rearrange("p (b k t) -> p b k t", b=2, k=KC)
                        [:, xb, :, :],
                        in_=xn_all.rearrange("(r k p) t -> r p k t", k=KC, p=128)
                        [r, :, :, co:co + 512]), sem=f"XNB{xb}", n=16,
                        waits=[("CCA", CC_AG), ("PE", xnb_rel[xb])])
                    for pr in ("q", "k", "v"):
                        for s in range(HPC):
                            for k in range(KC):
                                wts = [(f"XNB{xb}", ld), ("WQKV", GLD_W),
                                       ("SE", proj_rel[(pr, s)])] if k == 0 else ()
                                p.op("tensor", lambda e, pr=pr, s=s, k=k, xb=xb: e.matmul(
                                    psum_of[pr][:, s * 512:(s + 1) * 512],
                                    w_of[pr][:, (s * KC + k) * 128:(s * KC + k) * 128 + 128],
                                    xnb[:, (xb * KC + k) * 512:(xb * KC + k) * 512 + 512],
                                    start=(k == 0), stop=(k == KC - 1)),
                                    sem="PE" if k == KC - 1 else None, waits=wts)
                            pe_done = p.val("PE")
                            if pr == "q":
                                proj_rel[(pr, s)] = p.op("scalar",
                                    lambda e, s=s, tb=tb: e.activation(
                                        q_sb[:, s * SEQ + tb * 512:s * SEQ + tb * 512 + 512],
                                        p_q[:, s * 512:(s + 1) * 512], AF.Identity,
                                        bias=c_bqs[:, s:s + 1], scale=np.float32(ISCALE)),
                                    sem="SE", waits=[("PE", pe_done), ("VE", p.val("VE"))])
                            elif pr == "k":
                                proj_rel[(pr, s)] = p.op("scalar",
                                    lambda e, s=s, tb=tb: e.activation(
                                        k_sb[:, s * SEQ + tb * 512:s * SEQ + tb * 512 + 512],
                                        p_k[:, s * 512:(s + 1) * 512], AF.Identity,
                                        bias=c_bk[:, s:s + 1]),
                                    sem="SE", waits=[("PE", pe_done)])
                            else:
                                vwts = [("PE", pe_done), ("VE", vst_rel[s])]
                                if tb == 0 and s == 0:
                                    vwts += [("SP0", p.val("SP0")),
                                             ("SP1", p.val("SP1"))]
                                proj_rel[(pr, s)] = p.op("scalar",
                                    lambda e, s=s: e.activation(
                                        vstage[:, s * 512:(s + 1) * 512],
                                        p_v[:, s * 512:(s + 1) * 512], AF.Identity,
                                        bias=c_bv[:, s:s + 1]),
                                    sem="SE", waits=vwts)
                    xnb_rel[xb] = p.val("PE")
                    for s in range(HPC):
                        se_v = proj_rel[("v", s)]
                        for i in range(4):
                            p.op("tensor", lambda e, s=s, i=i: e.transpose(
                                p_vt[:, i * 128:(i + 1) * 128],
                                vstage[:, s * 512 + i * 128:s * 512 + i * 128 + 128],
                                c_id[:, :]), sem="PE" if i == 3 else None,
                                waits=([("SE", se_v), ("VE", p.val("VE"))] if i == 0 else ()))
                        tpe = p.val("PE")
                        vst_rel[s] = p.op("vector", lambda e, s=s, tb=tb: e.tensor_copy(
                            vt_sb[:, (s * 64 + tb * 4) * 128:(s * 64 + tb * 4) * 128 + 512],
                            p_vt[:, :]), sem="VE", waits=[("PE", tpe)])
                if "t_q" in taps:
                    p.op("sync", lambda e: e.dma_start(out=tap_d["t_q"], in_=q_sb[:, :]),
                         sem="TAP", n=16, waits=[("SE", p.val("SE"))])
                if "t_k" in taps:
                    p.op("sync", lambda e: e.dma_start(out=tap_d["t_k"], in_=k_sb[:, :]),
                         sem="TAP", n=16, waits=[("SE", p.val("SE"))])
                if "t_vt" in taps:
                    p.op("sync", lambda e: e.dma_start(out=tap_d["t_vt"], in_=vt_sb[:, :]),
                         sem="TAP", n=16, waits=[("VE", p.val("VE"))])
                p.flush(block)

            z_stack = ExitStack()
            if level >= 2:
                z_sb = z_stack.enter_context(
                    nc.sbuf_tensor("z_sb", [128, HPC * SEQ], BF16))

            if level >= 2:
              with ExitStack() as attn_ctx:
                AE = attn_ctx.enter_context
                pt_sb = AE(nc.sbuf_tensor("pt_sb", [128, 4 * 512], BF16))
                c_mask_s = AE(nc.sbuf_tensor("c_mask_s", [128, 4 * 512], BF16))
                mstats = AE(nc.sbuf_tensor("mstats", [128, 64], BF16))
                mcol = AE(nc.sbuf_tensor("mcol", [128, 1], F32))
                negm = AE(nc.sbuf_tensor("negm", [1, 2], F32))
                negmb = AE(nc.sbuf_tensor("negmb", [1, 2], BF16))
                biasM = AE(nc.sbuf_tensor("biasM", [128, 2], F32))
                sumr = AE(nc.sbuf_tensor("sumr", [1, 512], F32))
                rsum = AE(nc.sbuf_tensor("rsum", [1, 512], F32))
                bcsum = AE(nc.sbuf_tensor("bcsum", [128, 512], F32))
                sums_hist = AE(nc.sbuf_tensor("sums_hist", [1, 4 * 512], F32)) \
                    if "t_sums" in taps else None
                sums_slots = {14: 0, 15: 1, 30: 2, 31: 3}
                p_sa = AE(nc.psum_tensor("p_sa", [128, 1024], F32))
                p_st = AE(nc.psum_tensor("p_st", [128, 3 * 512], F32))
                p_z = AE(nc.psum_tensor("p_z", [128, 2 * 512], F32))
                p_row = AE(nc.psum_tensor("p_row", [33, 512], F32))

                p.barrier(["sync", "tensor", "scalar", "vector"])
                p.op("sync", lambda e: e.dma_start(out=c_mask_s[:, :], in_=c_mask_d),
                     sem="MASK", n=16)
                LD_MASK = p.val("MASK")

                state = {
                    "sa_rel": [0, 0],
                    "st_rel": [0, 0, 0],
                    "pt_rel": [0, 0, 0, 0],
                    "pz_rel": [("VE", 0), ("VE", 0)],
                    "prow_rel": [0, 0],
                    "pbc_rel": 0,
                    "mstats_rel": 0,
                    "biasM_ready": [0, 0],
                    "mcol_rel": 0,
                    "negmb_rel": [0, 0],
                    "mask_ok": LD_MASK,
                }

                def passA_units(i):
                    s, qc = divmod(i, NQC)
                    units = []
                    nblk = 0
                    blocks = []
                    for qt in range(4):
                        for kb in range(qc + 1):
                            blocks.append((qt, kb))
                    bi = 0
                    while bi < len(blocks):
                        pair = blocks[bi:bi + 2]
                        bi += len(pair)

                        def unit(pair=pair, i=i, s=s, qc=qc, base=nblk):
                            pe_vals = []
                            for j, (qt, kb) in enumerate(pair):
                                v = p.op("tensor", lambda e, s=s, qc=qc, qt=qt, kb=kb, j=j:
                                    e.matmul(p_sa[:, j * 512:(j + 1) * 512],
                                             q_sb[:, s * SEQ + qc * 512 + qt * 128:
                                                  s * SEQ + qc * 512 + qt * 128 + 128],
                                             k_sb[:, s * SEQ + kb * 512:
                                                  s * SEQ + kb * 512 + 512],
                                             start=True, stop=True),
                                    sem="PE",
                                    waits=[("VE", state["sa_rel"][j])])
                                pe_vals.append(v)
                            w = max(pe_vals)
                            width = 512 * len(pair)
                            r = p.op("vector", lambda e, base=base, width=width:
                                e.tensor_reduce(mstats[:, base:base + 1],
                                                p_sa[:, 0:width], AX, ALU.max),
                                sem="VE", waits=[("PE", w), ("VE", state["mstats_rel"])])
                            state["sa_rel"][0] = r
                            if len(pair) > 1:
                                state["sa_rel"][1] = r

                        units.append(unit)
                        nblk += 1

                    def finalize(i=i, nblk=nblk):
                        sl = i % 2
                        v1 = p.op("vector", lambda e, nblk=nblk: e.tensor_reduce(
                            mcol[:, :], mstats[:, 0:nblk], AX, ALU.max), sem="VE",
                            waits=[("VE", p.val("VE")), ("PE", state["mcol_rel"])])
                        state["mstats_rel"] = v1
                        t1 = p.op("tensor", lambda e: e.transpose(
                            p_sa[0:1, 0:128], mcol[:, :], c_idf[:, :]), sem="PE",
                            waits=([("C", C_ALL)] if i == 0 else []) +
                            [("VE", v1), ("VE", state["sa_rel"][0])])
                        state["mcol_rel"] = t1
                        v2 = p.op("vector", lambda e, sl=sl: e.tensor_reduce(
                            negm[:, sl:sl + 1], p_sa[0:1, 0:128], AX, ALU.max,
                            negate=True),
                            sem="VE", waits=[("PE", t1)])
                        state["sa_rel"][0] = v2
                        v3 = p.op("vector", lambda e, sl=sl: e.tensor_copy(
                            negmb[:, sl:sl + 1], negm[:, sl:sl + 1]), sem="VE",
                            waits=[("VE", v2), ("PE", state["negmb_rel"][sl])])
                        j = i % 2
                        t2 = p.op("tensor", lambda e, sl=sl, j=j: e.matmul(
                            p_z[:, j * 512:j * 512 + 1], c_on_rb[:, :],
                            negmb[:, sl:sl + 1],
                            start=True, stop=True), sem="PE",
                            waits=[("VE", v3), ("SE", state["pbc_rel"]),
                                   state["pz_rel"][j]])
                        state["negmb_rel"][sl] = t2
                        state["biasM_ready"][sl] = p.op("scalar",
                            lambda e, sl=sl, j=j: e.activation(
                                biasM[:, sl:sl + 1], p_z[:, j * 512:j * 512 + 1],
                                AF.Copy),
                            sem="SE", waits=[("PE", t2)])
                        state["pbc_rel"] = state["biasM_ready"][sl]
                        state["pz_rel"][j] = ("SE", state["biasM_ready"][sl])

                    return units, finalize

                def passB_units(i):
                    s, qc = divmod(i, NQC)
                    nkt = (qc + 1) * 4
                    qoff = s * SEQ + qc * 512
                    zsl = i % 2
                    units = []

                    def st_mm(kt):
                        sl = kt % 3
                        wts = [("SE", state["st_rel"][sl])]
                        return p.op("tensor", lambda e, kt=kt, sl=sl: e.matmul(
                            p_st[:, sl * 512:(sl + 1) * 512],
                            k_sb[:, s * SEQ + kt * 128:s * SEQ + kt * 128 + 128],
                            q_sb[:, qoff:qoff + 512], start=True, stop=True),
                            sem="PE", waits=wts)

                    pe_st = {}

                    def unit(kt, nkt=nkt, s=s, qc=qc, i=i):
                        if kt == 0:
                            pe_st[0] = st_mm(0)
                            if nkt > 1:
                                pe_st[1] = st_mm(1)
                        sl = kt % 3
                        ps = kt % 4
                        se = p.op("scalar", lambda e, kt=kt, sl=sl, ps=ps, i=i:
                            e.activation(pt_sb[:, ps * 512:(ps + 1) * 512],
                                         p_st[:, sl * 512:(sl + 1) * 512], AF.Exp,
                                         bias=biasM[:, i % 2:i % 2 + 1]),
                            sem="SE", waits=[("PE", pe_st[kt]),
                                             ("PE", state["pt_rel"][ps]),
                                             ("SE", state["biasM_ready"][i % 2])])
                        state["st_rel"][sl] = se
                        ready = ("SE", se)
                        i_d = kt - qc * 4
                        if i_d >= 0:
                            dv = p.op("vector", lambda e, ps=ps, i_d=i_d: e.tensor_tensor(
                                pt_sb[:, ps * 512:(ps + 1) * 512],
                                pt_sb[:, ps * 512:(ps + 1) * 512],
                                c_mask_s[:, i_d * 512:(i_d + 1) * 512], ALU.mult),
                                sem="VE", waits=[("SE", se), ("MASK", state["mask_ok"])])
                            ready = ("VE", dv)
                        if kt + 2 < nkt:
                            pe_st[kt + 2] = st_mm(kt + 2)
                        wts = [ready]
                        if kt == 0:
                            wts.append(("VE", state["prow_rel"][zsl]))
                        p.op("tensor", lambda e, kt=kt, ps=ps, nkt=nkt, zsl=zsl: e.matmul(
                            p_row[32 * zsl:32 * zsl + 1, :], c_on_c[:, :],
                            pt_sb[:, ps * 512:(ps + 1) * 512],
                            start=(kt == 0), stop=(kt == nkt - 1)),
                            sem=None, waits=wts)
                        wts2 = []
                        if kt == 0:
                            wts2.append(state["pz_rel"][zsl])
                        state["pt_rel"][ps] = p.op("tensor",
                            lambda e, kt=kt, ps=ps, s=s, nkt=nkt, zsl=zsl: e.matmul(
                                p_z[:, zsl * 512:(zsl + 1) * 512],
                                vt_sb[:, (s * 64 + kt) * 128:(s * 64 + kt) * 128 + 128],
                                pt_sb[:, ps * 512:(ps + 1) * 512],
                                start=(kt == 0), stop=(kt == nkt - 1)),
                            sem="PE", waits=wts2)

                    for kt in range(nkt):
                        units.append(lambda kt=kt: unit(kt))

                    def finalize(i=i, s=s, qc=qc, qoff=qoff, zsl=None):
                        zsl = i % 2
                        pe_last = p.val("PE")
                        v1 = p.op("vector", lambda e, zsl=zsl: e.tensor_copy(
                            sumr[:, :], p_row[32 * zsl:32 * zsl + 1, :]),
                            sem="VE", waits=[("PE", pe_last)])
                        state["prow_rel"][zsl] = v1
                        if sums_hist is not None and i in sums_slots:
                            sslot = sums_slots[i]
                            p.op("vector", lambda e, sslot=sslot: e.tensor_copy(
                                sums_hist[:, sslot * 512:(sslot + 1) * 512], sumr[:, :]),
                                sem="VE", waits=[("VE", v1)])
                        v2 = p.op("vector", lambda e: e.reciprocal(rsum[:, :], sumr[:, :]),
                                  sem="VE", waits=[("VE", v1)])
                        j = (i + 1) % 2
                        t1 = p.op("tensor", lambda e, j=j: e.matmul(
                            p_z[:, j * 512:(j + 1) * 512], c_on_rf[:, :], rsum[:, :],
                            start=True, stop=True),
                            sem="PE", waits=[("VE", v2), ("SE", state["pbc_rel"]),
                                             state["pz_rel"][j]])
                        s1 = p.op("scalar", lambda e, j=j: e.activation(
                            bcsum[:, :], p_z[:, j * 512:(j + 1) * 512], AF.Copy),
                            sem="SE", waits=[("PE", t1)])
                        state["pbc_rel"] = s1
                        state["pz_rel"][j] = ("SE", s1)
                        v3 = p.op("vector", lambda e, qoff=qoff, zsl=zsl: e.tensor_tensor(
                            z_sb[:, qoff:qoff + 512], p_z[:, zsl * 512:(zsl + 1) * 512],
                            bcsum[:, :], ALU.mult),
                            sem="VE", waits=[("SE", s1)])
                        state["pz_rel"][zsl] = ("VE", v3)

                    return units, finalize

                npair = HPC * NQC
                prevB = None
                pendingA_fin = None
                ua, fa = passA_units(0)
                for u in ua:
                    u()
                fa()
                for i in range(npair):
                    ub, fb = passB_units(i)
                    if i + 1 < npair:
                        ua, fa = passA_units(i + 1)
                    else:
                        ua, fa = [], None
                    if interleave:
                        nb, na = len(ub), len(ua)
                        ai = 0
                        for j, u in enumerate(ub):
                            u()
                            while ai < na and ai * nb < (j + 1) * na:
                                ua[ai]()
                                ai += 1
                            if ai == na and fa is not None:
                                fa()
                                fa = None
                        while ai < na:
                            ua[ai]()
                            ai += 1
                        if fa is not None:
                            fa()
                        fb()
                    else:
                        for u in ub:
                            u()
                        fb()
                        for u in ua:
                            u()
                        if fa is not None:
                            fa()

                if "t_z" in taps:
                    p.op("sync", lambda e: e.dma_start(out=tap_d["t_z"], in_=z_sb[:, :]),
                         sem="TAP", n=16, waits=[("VE", p.val("VE"))])
                if sums_hist is not None:
                    p.op("sync", lambda e: e.dma_start(out=tap_d["t_sums"], in_=sums_hist[:, :]),
                         sem="TAP", n=16, waits=[("VE", p.val("VE"))])
                p.flush(block)

            bar_attn_end = [(s, p.val(s)) for s in ("PE", "SE", "VE")]

            if level >= 3:
              with ExitStack() as wo_ctx:
                WE = wo_ctx.enter_context
                wo_sb = WE(nc.sbuf_tensor("wo_sb", [128, HPC * D], BF16))
                rs_stage = WE(nc.sbuf_tensor("rs_stage", [128, 2 * D], BF16))
                p_at = WE(nc.psum_tensor("p_at", [128, 2 * D], F32))
                p.barrier(["gpsimd", "tensor", "scalar", "sync"])
                p.op("gpsimd", lambda e: e.dma_start(out=wo_sb[:, :], in_=wo),
                     sem="WOL", n=16)
                GLD_WO = p.val("WOL")
                at_rel = [0, 0]
                stg_rel = [0, 0]
                for gt in range(SEQ // 128):
                    g = gt % 2
                    for nn in range(4):
                        for s in range(HPC):
                            wts = []
                            if nn == 0 and s == 0:
                                wts = [("SE", at_rel[g]), ("WOL", GLD_WO)]
                            p.op("tensor", lambda e, gt=gt, g=g, nn=nn, s=s: e.matmul(
                                p_at[:, g * D + nn * 512:g * D + nn * 512 + 512],
                                z_sb[:, s * SEQ + gt * 128:s * SEQ + gt * 128 + 128],
                                wo_sb[:, s * D + nn * 512:s * D + nn * 512 + 512],
                                start=(s == 0), stop=(s == HPC - 1)),
                                sem="PE" if (nn == 3 and s == HPC - 1) else None,
                                waits=wts)
                    pe_gt = p.val("PE")
                    at_rel[g] = p.op("scalar", lambda e, g=g: e.activation(
                        rs_stage[:, g * D:(g + 1) * D], p_at[:, g * D:(g + 1) * D],
                        AF.Copy), sem="SE",
                        waits=[("PE", pe_gt), (f"SR{g}", stg_rel[g])])
                    stg_rel[g] = p.op("sync", lambda e, gt=gt, g=g: e.dma_start(
                        out=rs_ctb[gt * 128:(gt + 1) * 128, :],
                        in_=rs_stage[:, g * D:(g + 1) * D]), sem=f"SR{g}", n=16,
                        waits=[("SE", at_rel[g])])
                SR0_ALL, SR1_ALL = p.val("SR0"), p.val("SR1")
                p.op("gpsimd", lambda e: e.collective_compute(
                    "ReduceScatter", ALU.add, replica_groups=[list(range(NCORE))],
                    ins=[rs_ctb.opt()], outs=[rs_sc.opt()]), sem="CCR", n=1,
                    waits=[("SR0", SR0_ALL), ("SR1", SR1_ALL)])
                CC_RS = p.val("CCR")
                if "t_rs" in taps:
                    p.op("sync", lambda e: e.dma_start(out=tap_d["t_rs"], in_=rs_sc),
                         sem="TAP", n=16, waits=[("CCR", CC_RS)])
                p.flush(block)
            z_stack.close()

        if level >= 4:
            emit_ffn([1], add_rs=True, bar=None)

        for s in ("STA", "SR0", "SR1", "SP0", "SP1", "SO0", "SO1", "TAP"):
            if s in p.cnt:
                p.wait("sync", s, p.val(s))
        p.flush(block)

    return nc



def _prep_inputs(L0_attn_x, L0_attn_W_Q, L0_attn_W_K, L0_attn_W_V, L0_attn_W_O,
                 L0_attn_b_Q, L0_attn_b_K, L0_attn_b_V, L0_attn_b_O,
                 L0_ffn_w_gate, L0_ffn_w_up, L0_ffn_w_down,
                 L0_ffn_s_gate, L0_ffn_s_up, L0_ffn_s_down):
    x = np.asarray(L0_attn_x, np.float32).reshape(SEQ, D)
    WQ = np.asarray(L0_attn_W_Q, np.float32)
    WK = np.asarray(L0_attn_W_K, np.float32)
    WV = np.asarray(L0_attn_W_V, np.float32)
    WO = np.asarray(L0_attn_W_O, np.float32)
    bQ = np.asarray(L0_attn_b_Q, np.float32)
    bK = np.asarray(L0_attn_b_K, np.float32)
    bV = np.asarray(L0_attn_b_V, np.float32)
    bO = np.asarray(L0_attn_b_O, np.float32).reshape(1, D)
    wg = np.asarray(L0_ffn_w_gate, np.int8)
    wu = np.asarray(L0_ffn_w_up, np.int8)
    wd = np.asarray(L0_ffn_w_down, np.int8)
    sgv = np.asarray(L0_ffn_s_gate, np.float32)
    suv = np.asarray(L0_ffn_s_up, np.float32)
    sd = np.asarray(L0_ffn_s_down, np.float32).reshape(1, D)

    consts = dict(
        c_id=np.eye(128, dtype=ml_dtypes.bfloat16),
        c_idf=np.eye(128, dtype=np.float32),
        c_on_c=np.ones((128, 1), ml_dtypes.bfloat16),
        c_on_rb=np.ones((1, 128), ml_dtypes.bfloat16),
        c_on_rf=np.ones((1, 128), np.float32),
        sg_i=np.ascontiguousarray(sgv.reshape(64, 128).T),
        su_i=np.ascontiguousarray(suv.reshape(64, 128).T),
        bo_r=bO, sd_r=sd, wg_d=wg, wu_d=wu, wd_d=wd,
    )
    mask = np.zeros((128, 4 * 512), ml_dtypes.bfloat16)
    row = np.arange(128)[:, None]
    col = np.arange(512)[None, :]
    for i in range(4):
        mask[:, i * 512:(i + 1) * 512] = (col >= i * 128 + row).astype(
            ml_dtypes.bfloat16)
    consts["c_mask"] = mask

    in_maps = []
    for c in range(NCORE):
        hs = slice(HPC * c, HPC * (c + 1))
        m = dict(consts)
        m["x_sh"] = np.ascontiguousarray(x[c * TOK:(c + 1) * TOK])
        m["wq"] = np.ascontiguousarray(
            WQ[hs].reshape(HPC, KC, 128, 128).transpose(2, 0, 1, 3).reshape(128, -1))
        m["wk"] = np.ascontiguousarray(
            WK[hs].reshape(HPC, KC, 128, 128).transpose(2, 0, 1, 3).reshape(128, -1))
        m["wv"] = np.ascontiguousarray(
            WV[hs].reshape(HPC, KC, 128, 128).transpose(2, 0, 1, 3).reshape(128, -1))
        m["wo"] = np.ascontiguousarray(
            WO[hs].transpose(1, 0, 2).reshape(128, -1))
        m["bq"] = np.ascontiguousarray(bQ[hs].T)
        m["bk"] = np.ascontiguousarray(bK[hs].T)
        m["bv"] = np.ascontiguousarray(bV[hs].T)
        in_maps.append(m)
    return in_maps


_NC_CACHE = {}


def kernel(**inputs):
    in_maps = _prep_inputs(**inputs)
    key = "main"
    if key not in _NC_CACHE:
        _NC_CACHE[key] = build()
    nc = _NC_CACHE[key]
    res = run_bass_kernel_spmd(nc, in_maps, core_ids=list(range(NCORE)))
    out = np.concatenate([np.asarray(r["out"]) for r in res.results], axis=0)
    return out.reshape(1, SEQ, D)
